# version 07
# baseline (speedup 1.0000x reference)
import math
from contextlib import ExitStack

import numpy as np
import ml_dtypes

import concourse.bass as bass
import concourse.mybir as mybir
from concourse.bass_utils import run_bass_kernel_spmd

F32 = mybir.dt.float32
F16 = mybir.dt.float16
BF16 = mybir.dt.bfloat16
AF = mybir.ActivationFunctionType
ALU = mybir.AluOpType
AXX = mybir.AxisListType.X

D = 1024
T = 2048
NDT = 8
NG = 8
H = 16
DH = 64
TCH = 512
NTC = 4
DNC = 128
NCH = 16
PADI = 2816
NIC = 22
FFN_CHUNKS = (8, 8, 6)
WIN = 256
NW = 8
NEG = -1e30

CFG = {
    "tap": None,
    "layers": 6,
    "do_dn": True,
    "batches": 2,
}



def _rotary_tables():
    half = np.arange(0, DH, 2, dtype=np.float64)
    inv_freq = 1.0 / 10000.0 ** (half / DH)
    t = np.arange(T, dtype=np.float64)
    freqs = t[:, None] * inv_freq[None, :]
    freqs = np.concatenate([freqs, freqs], -1)
    base = (half + 0.4 * DH) / (1.4 * DH)
    power = (t - T // 2) / (WIN // 2)
    scale = base[None, :] ** power[:, None]
    scale = np.concatenate([scale, scale], -1)
    cs_q = (np.cos(freqs) * scale * 8.0).T
    ss_q = (np.sin(freqs) * scale * 8.0).T
    cs_k = (np.cos(freqs) / scale).T
    ss_k = (np.sin(freqs) / scale).T
    out = {}
    for nm, a in [("csq", cs_q), ("ssq", ss_q), ("csk", cs_k), ("ssk", ss_k)]:
        out[nm] = np.ascontiguousarray(
            np.tile(a, (2, 1)).astype(ml_dtypes.bfloat16))
    return out


def _consts():
    c = {}
    c.update(_rotary_tables())
    c["ident16"] = np.eye(128, dtype=np.float16)
    c["identbf"] = np.eye(128, dtype=ml_dtypes.bfloat16)
    c["ones_col"] = np.ones((128, 1), dtype=np.float16)
    c["ones_row"] = np.ones((1, 128), dtype=np.float16)
    c["ones_rowb"] = np.ones((1, 128), dtype=ml_dtypes.bfloat16)
    c["ones_row32"] = np.ones((1, 128), dtype=np.float32)
    s2 = np.zeros((128, 2), dtype=np.float16)
    s2[0:64, 0] = 1; s2[64:128, 1] = 1
    c["sel2"] = s2
    i2 = np.zeros((2, 128), dtype=ml_dtypes.bfloat16)
    i2[0, 0:64] = 1; i2[1, 64:128] = 1
    c["ind2b"] = i2
    c["ind2h"] = i2.astype(np.float16)
    hs = np.zeros((16, 1024), dtype=np.float16)
    for i in range(1024):
        hs[i // 64, i] = 1
    c["head_sel"] = hs
    R = np.zeros((64, 64), dtype=np.float64)
    for i in range(32):
        R[i, i + 32] = -1.0
        R[i + 32, i] = 1.0
    RT = R.T
    rb = np.zeros((128, 128), dtype=np.float64)
    rb[0:64, 0:64] = RT; rb[64:128, 64:128] = RT
    c["rsgnT"] = rb.astype(ml_dtypes.bfloat16)
    tl = np.tril(np.ones((128, 128)), -1)
    tu = np.triu(np.ones((128, 128)), 1)
    tud = np.triu(np.ones((128, 128)), 0)
    c["mlow"] = np.concatenate([tl, tl], 1).astype(np.float16)
    c["mup"] = np.concatenate([tu, tu], 1).astype(np.float16)
    c["mupd"] = np.concatenate([tud, tud], 1).astype(np.float16)
    def amask(qi, ncols, kofs):
        m = np.zeros((128, ncols), dtype=np.float64)
        for r in range(128):
            i = 128 * qi + r + 256
            j = np.arange(ncols) + kofs
            m[r] = np.where((i - j < 0) | (i - j > 256), NEG, 0.0)
        return m
    c["maskA0"] = amask(0, 512, 0).astype(ml_dtypes.bfloat16)
    c["maskA1"] = amask(1, 512, 0).astype(ml_dtypes.bfloat16)
    def amask0(qi):
        m = np.zeros((128, 256), dtype=np.float64)
        for r in range(128):
            i = 128 * qi + r
            j = np.arange(256)
            m[r] = np.where((i - j < 0) | (i - j > 256), NEG, 0.0)
        return m
    c["maskB0"] = amask0(0).astype(ml_dtypes.bfloat16)
    c["maskB1"] = amask0(1).astype(ml_dtypes.bfloat16)
    c["fvals"] = np.tile(np.array([[1e-5, -0.5, 1.0 / 1024.0, 1e12]],
                                  dtype=np.float32), (128, 1))
    return c


def _prep_weights(inp):
    w = {}
    t128 = lambda a: np.ascontiguousarray(
        a.reshape(8, 128, a.shape[-1]).transpose(1, 0, 2))
    w["dn_wq"] = t128(np.asarray(inp["L0_attn_Wq"], np.float32))
    w["dn_wk"] = t128(np.asarray(inp["L0_attn_Wk"], np.float32))
    w["dn_wv"] = t128(np.asarray(inp["L0_attn_Wv"], np.float32))
    w["dn_wout"] = t128(np.asarray(inp["L0_attn_Wout"], np.float32))
    w["dn_wbeta"] = t128(np.asarray(inp["L0_attn_Wbeta"], np.float32))
    bb = np.asarray(inp["L0_attn_b_beta"], np.float32)
    w["dn_bbeta_col"] = np.ascontiguousarray(bb.reshape(16, 1))
    w["dn_bbeta_row"] = np.ascontiguousarray(bb.reshape(1, 16))

    wqkv = np.asarray(inp["L0_ffn_Wqkv"], np.float32)
    t128L = lambda a: np.ascontiguousarray(
        a.reshape(6, 8, 128, a.shape[-1]).transpose(0, 2, 1, 3))
    w["st_wq"] = t128L(wqkv[:, :, 0:1024])
    w["st_wk"] = t128L(wqkv[:, :, 1024:2048])
    w["st_wv"] = t128L(wqkv[:, :, 2048:3072])
    w["st_wo"] = t128L(np.asarray(inp["L0_ffn_Wo"], np.float32))
    W1 = np.asarray(inp["L0_ffn_W1"], np.float32)
    DI = 2730
    W1a = np.zeros((6, 1024, PADI), np.float32); W1a[:, :, :DI] = W1[:, :, :DI]
    W1g = np.zeros((6, 1024, PADI), np.float32); W1g[:, :, :DI] = W1[:, :, DI:]
    tW1 = lambda a: np.ascontiguousarray(
        a.reshape(6, 8, 128, 22, 128).transpose(0, 2, 3, 1, 4))
    w["st_w1a"] = tW1(W1a)
    w["st_w1g"] = tW1(W1g)
    W2 = np.asarray(inp["L0_ffn_W2"], np.float32)
    W2p = np.zeros((6, PADI, 1024), np.float32); W2p[:, :DI] = W2
    w["st_w2"] = np.ascontiguousarray(
        W2p.reshape(6, 22, 128, 8, 128).transpose(0, 2, 1, 3, 4))
    b1 = np.asarray(inp["L0_ffn_b1"], np.float32)
    b1a = np.zeros((6, PADI), np.float32); b1a[:, :DI] = b1[:, :DI]
    b1g = np.zeros((6, PADI), np.float32); b1g[:, :DI] = b1[:, DI:]
    w["st_b1a"] = np.ascontiguousarray(b1a.reshape(6, 22, 128).transpose(0, 2, 1))
    w["st_b1g"] = np.ascontiguousarray(b1g.reshape(6, 22, 128).transpose(0, 2, 1))
    col8 = lambda a: np.ascontiguousarray(a.reshape(6, 8, 128).transpose(0, 2, 1))
    w["st_lng"] = col8(np.asarray(inp["L0_ffn_ln_g"], np.float32))
    w["st_lnb"] = col8(np.asarray(inp["L0_ffn_ln_b"], np.float32))
    w["st_b2"] = col8(np.asarray(inp["L0_ffn_b2"], np.float32))
    qs = np.asarray(inp["L0_ffn_q_scale"], np.float32)
    ks = np.asarray(inp["L0_ffn_k_scale"], np.float32)
    w["st_qs"] = np.ascontiguousarray(np.tile(qs, (1, 2)).reshape(6, 128, 1))
    w["st_ks"] = np.ascontiguousarray(np.tile(ks, (1, 2)).reshape(6, 128, 1))
    return w



class Ch:
    def __init__(self, es, nc, name, step):
        self.sem = es.enter_context(nc.semaphore(name))
        self.step = step
        self.n = 0
    @property
    def val(self):
        return self.n * self.step
    def tick(self, instr):
        instr.then_inc(self.sem, self.step)
        self.n += 1
        return self.val


class P:
    def __init__(self, nc, block, es):
        self.nc = nc
        self.block = block
        self.es = es
        self.ct = Ch(es, nc, "ct", 1)
        self.cv = Ch(es, nc, "cv", 1)
        self.cs = Ch(es, nc, "cs", 1)
        self.cp = Ch(es, nc, "cp", 1)
        self.dW = Ch(es, nc, "dW", 16)
        self.dV = Ch(es, nc, "dV", 16)
        self.dX = Ch(es, nc, "dX", 16)
        self.dP = Ch(es, nc, "dP", 16)
        self.dK = Ch(es, nc, "dK", 16)
        self.done = False

    def bar(self):
        if self.done:
            return
        b = self.block
        vals = {}
        chs = [("tensor", self.ct), ("vector", self.cv),
               ("scalar", self.cs), ("gpsimd", self.cp)]
        for name, ch in chs:
            def mk(ch):
                def f(e):
                    ch.tick(e.drain())
                return f
            getattr(b, name)(mk(ch))
            vals[name] = ch.val
        for name, ch in chs:
            def mk2(vals, me):
                def f(e):
                    for nm2, ch2 in chs:
                        if nm2 != me:
                            e.wait_ge(ch2.sem, vals[nm2])
                return f
            getattr(b, name)(mk2(vals, name))


def _act_set_idx(nc, funcs):
    from concourse.hw_specs import get_activation_tables
    for i, fs in enumerate(get_activation_tables(nc.m.arch).values()):
        if all(f in fs for f in funcs):
            return i
    raise ValueError(f"no act set with {funcs}")


def build_program(CFGL):
    nc = bass.Bass("TRN2", target_bir_lowering=False)
    LNEXP_SET = _act_set_idx(nc, (AF.Ln, AF.Exp, AF.Copy))

    def preload_lnexp(s):
        s.add_instruction(mybir.InstLoadActFuncSet(
            name=nc.get_next_instruction_name(),
            act_func_set_id=LNEXP_SET, ins=[], outs=[]))
    es = ExitStack()

    dt = {}
    def din(name, shape, dty=F32):
        dt[name] = nc.dram_tensor(name, list(shape), dty, kind="ExternalInput").ap()
    din("xin", (2, D, T))
    din("dn_wq", (128, 8, 1024)); din("dn_wk", (128, 8, 1024))
    din("dn_wv", (128, 8, 1024)); din("dn_wout", (128, 8, 1024))
    din("dn_wbeta", (128, 8, 16))
    din("dn_bbeta_col", (16, 1)); din("dn_bbeta_row", (1, 16))
    din("st_wq", (6, 128, 8, 1024)); din("st_wk", (6, 128, 8, 1024))
    din("st_wv", (6, 128, 8, 1024)); din("st_wo", (6, 128, 8, 1024))
    din("st_w1a", (6, 128, 22, 8, 128)); din("st_w1g", (6, 128, 22, 8, 128))
    din("st_w2", (6, 128, 22, 8, 128))
    din("st_b1a", (6, 128, 22)); din("st_b1g", (6, 128, 22))
    din("st_lng", (6, 128, 8)); din("st_lnb", (6, 128, 8)); din("st_b2", (6, 128, 8))
    din("st_qs", (6, 128, 1)); din("st_ks", (6, 128, 1))
    din("csq", (128, T), BF16); din("ssq", (128, T), BF16)
    din("csk", (128, T), BF16); din("ssk", (128, T), BF16)
    din("ident16", (128, 128), F16); din("identbf", (128, 128), BF16)
    din("ones_col", (128, 1), F16); din("ones_row", (1, 128), F16)
    din("ones_rowb", (1, 128), BF16)
    din("ones_row32", (1, 128), F32)
    din("sel2", (128, 2), F16); din("ind2b", (2, 128), BF16)
    din("ind2h", (2, 128), F16)
    din("head_sel", (16, 1024), F16)
    din("rsgnT", (128, 128), BF16)
    din("mlow", (128, 256), F16); din("mup", (128, 256), F16)
    din("mupd", (128, 256), F16)
    din("maskA0", (128, 512), BF16); din("maskA1", (128, 512), BF16)
    din("maskB0", (128, 256), BF16); din("maskB1", (128, 256), BF16)
    din("fvals", (128, 4), F32)

    xout = nc.dram_tensor("xout", [2, D, T], F32, kind="ExternalOutput").ap()
    dbg = nc.dram_tensor("dbg", [128, 16448], F32, kind="ExternalOutput").ap()
    x1park = nc.dram_tensor("x1park", [2, D, T], F16, kind="Internal").ap()

    def sb(name, shape, dty):
        return es.enter_context(nc.sbuf_tensor(name, list(shape), dty))
    X = sb("X", (128, 8 * (T + 1)), F16)
    XN = sb("XN", (128, 8 * T), F16)
    ASC16 = sb("ASC16", (128, 10240), F16)
    ASCB = sb("ASCB", (128, 10240), BF16)
    WAR = sb("WAR", (128, 32768), F16)
    SCR = sb("SCR", (128, 4352), F16)
    SCRB = sb("SCRB", (128, 1024), BF16)
    TBL = sb("TBL", (128, 4 * 512), BF16)
    ROWS = sb("ROWS", (2, 1536), F32)
    ROWB = sb("ROWB", (1, 1024), F16)
    BHM = sb("BHM", (16, T), F16)
    BTM = sb("BTM", (128, 256), F32)
    BB = sb("BB", (128, 16), F32)
    X1S = sb("X1S", (128, 512), F16)
    CONST = {}
    cshapes = {
        "ident16": ((128, 128), F16), "identbf": ((128, 128), BF16),
        "ones_col": ((128, 1), F16), "ones_row": ((1, 128), F16),
        "ones_rowb": ((1, 128), BF16), "ones_row32": ((1, 128), F32),
        "sel2": ((128, 2), F16), "ind2b": ((2, 128), BF16),
        "ind2h": ((2, 128), F16),
        "head_sel": ((16, 1024), F16), "rsgnT": ((128, 128), BF16),
        "mlow": ((128, 256), F16), "mup": ((128, 256), F16),
        "mupd": ((128, 256), F16),
        "maskA0": ((128, 512), BF16), "maskA1": ((128, 512), BF16),
        "maskB0": ((128, 256), BF16), "maskB1": ((128, 256), BF16),
        "bbeta_col": ((16, 1), F32), "bbeta_row": ((1, 16), F32),
        "fvals": ((128, 4), F32),
    }
    for nm, (shp, dty) in cshapes.items():
        CONST[nm] = sb("c_" + nm, shp, dty)
    B1A = sb("B1A", (128, 22), F32); B1G = sb("B1G", (128, 22), F32)
    LNG = sb("LNG", (128, 8), F32); LNB = sb("LNB", (128, 8), F32)
    B2C = sb("B2C", (128, 8), F32)
    QSC = sb("QSC", (128, 1), F32); KSC = sb("KSC", (128, 1), F32)

    pb = [es.enter_context(nc.psum_tensor(f"pb{i}", [128, 512], F32))
          for i in range(7)]
    pbT = es.enter_context(nc.psum_tensor("pbT", [128, 1024], F16))

    def xsl(dtile, t0, n, shift=0):
        base = dtile * (T + 1) + 1 + t0 - shift
        return X[:, base:base + n]

    def xnsl(dtile, t0, n):
        return XN[:, dtile * T + t0:dtile * T + t0 + n]

    v_roll = lambda s: ASC16[:, s * 2048:(s + 1) * 2048]
    oT_sl = lambda: ASC16[:, 6144:8192]
    dn_q = lambda: ASC16[:, 0:2048]
    dn_kn = lambda: ASC16[:, 2048:4096]
    dn_vt = lambda: ASC16[:, 4096:6144]
    dn_oT = lambda g2: ASC16[:, 6144 + g2 * 2048: 8192 + g2 * 2048]
    ffn_h = lambda: ASC16[:, 0:8 * 512]
    q_rot = lambda s: ASCB[:, s * 2048:(s + 1) * 2048]
    k_roll = lambda s: ASCB[:, 4096 + s * 2048: 6144 + s * 2048]
    war = lambda o, n: WAR[:, o:o + n]

    block = es.enter_context(nc.Block())
    p = P(nc, block, es)

    taps = []

    def tap(name, ap_f32src=None, f=None):
        if CFGL["tap"] == name and not p.done:
            f()
            p.done = True

    def dump(src_ap, cols, nparts=128):
        stage = SCR[:, 0:512]
        nchunks = (cols + 511) // 512
        for i in range(nchunks):
            n = min(512, cols - i * 512)
            @block.vector
            def _(v, i=i, n=n):
                v.wait_ge(p.ct.sem, p.ct.val)
                v.wait_ge(p.cs.sem, p.cs.val)
                v.wait_ge(p.dW.sem, p.dW.val)
                v.wait_ge(p.cv.sem, p.cv.val)
                v.drain()
                p.cv.tick(v.tensor_copy(stage[0:nparts, 0:n],
                                        src_ap[:, i * 512:i * 512 + n]))
            @block.gpsimd
            def _(g, i=i, n=n):
                g.wait_ge(p.cv.sem, p.cv.val)
                p.dW.tick(g.dma_start(out=dbg[0:nparts, i * 512:i * 512 + n],
                                      in_=stage[0:nparts, 0:n]))
            @block.vector
            def _(v):
                v.wait_ge(p.dW.sem, p.dW.val)

    @block.sync
    def _(sy):
        for nm in cshapes:
            src = dt["dn_" + nm] if nm.startswith("bbeta") else dt[nm]
            p.dX.tick(sy.dma_start(out=CONST[nm][:], in_=src[:]))
    consts_done = p.dX.val
    for i, v in enumerate([1e-5, -0.5, 1.0 / 1024.0, 1e12]):
        nc.const_aps.register(F32, v, CONST["fvals"][:, i:i + 1])
    for engn in ("tensor", "vector", "scalar", "gpsimd"):
        def mkw(engn=engn):
            def f(e):
                e.wait_ge(p.dX.sem, consts_done)
            return f
        getattr(block, engn)(mkw())

    @block.vector
    def _(v):
        v.memset(X[:], 0.0)
        p.cv.tick(v.memset(BTM[:], 0.0))
    xzero_done = p.cv.val

    @block.tensor
    def _(t):
        t.wait_ge(p.dX.sem, consts_done)
        p.ct.tick(t.matmul(pb[0][:, 0:16], CONST["ones_row32"][:],
                           CONST["bbeta_row"][:], start=True, stop=True))
    @block.scalar
    def _(s):
        s.wait_ge(p.ct.sem, p.ct.val)
        p.cs.tick(s.copy(BB[:], pb[0][:, 0:16]))
    bb_done = p.cs.val
    p.bar()

    def emit_load_x(b):
        @block.gpsimd
        def _(g):
            g.wait_ge(p.cv.sem, max(xzero_done, p.cv.val))
            g.wait_ge(p.dW.sem, p.dW.val)
            for dtile in range(NDT):
                p.dW.tick(g.dma_start(out=xsl(dtile, 0, T),
                                      in_=dt["xin"][b, dtile * 128:(dtile + 1) * 128, :]))
        return p.dW.val

    def emit_ln(src_shift, dst, gb, xload_val=None, wait_dX=None):
        sq = SCR[:, 0:512]
        sq2 = SCR[:, 512:1024]
        prev_mm_last = None
        tcs = CFGL.get("ln_tcs", range(NTC))
        for tc in tcs:
            t0 = tc * TCH
            sq_done = []
            for dtile in range(NDT):
                buf = sq if dtile % 2 == 0 else sq2
                @block.vector
                def _(v, dtile=dtile, buf=buf, tc=tc):
                    if xload_val is not None and tc == 0 and dtile == 0:
                        v.wait_ge(p.dW.sem, xload_val)
                    if dtile >= 2:
                        v.wait_ge(p.ct.sem, mm_done[dtile - 2])
                    elif prev_mm_last is not None:
                        v.wait_ge(p.ct.sem, prev_mm_last)
                    xa = xsl(dtile, t0, TCH, src_shift and dtile >= 4)
                    p.cv.tick(v.tensor_tensor(buf, xa, xa, ALU.mult))
                sq_done.append(p.cv.val)
                @block.tensor
                def _(t, dtile=dtile, buf=buf):
                    if dtile == 0 and xload_val is not None and tc == 0:
                        t.wait_ge(p.dW.sem, xload_val)
                    t.matmul(pb[0][0:1, :], CONST["ones_col"][:],
                             xsl(dtile, t0, TCH, src_shift and dtile >= 4),
                             start=(dtile == 0), stop=(dtile == NDT - 1))
                    t.wait_ge(p.cv.sem, sq_done[dtile])
                    p.ct.tick(t.matmul(pb[1][0:1, :], CONST["ones_col"][:], buf,
                             start=(dtile == 0), stop=(dtile == NDT - 1)))
                if dtile == 0:
                    mm_done = []
                mm_done.append(p.ct.val)
            mu = ROWS[0:1, 0:512]
            var = ROWS[0:1, 512:1024]
            rstd_b = ROWB[0:1, 0:512]
            murstd_b = ROWB[0:1, 512:1024]
            @block.vector
            def _(v):
                v.wait_ge(p.ct.sem, mm_done[-1])
                v.tensor_scalar_mul(mu, pb[0][0:1, :], 1.0 / 1024.0)
                v.tensor_scalar_mul(var, pb[1][0:1, :], 1.0 / 1024.0)
                v.drain()
                v.tensor_tensor(ROWS[0:1, 1024:1536], mu, mu, ALU.mult)
                v.drain()
                p.cv.tick(v.tensor_tensor(var, var, ROWS[0:1, 1024:1536],
                                          ALU.subtract))
            row_v = p.cv.val
            @block.scalar
            def _(s):
                s.wait_ge(p.cv.sem, row_v)
                s.activation(var, var, AF.Ln, bias=1e-5)
                s.drain()
                p.cs.tick(s.activation(ROWS[0:1, 1024:1536], var, AF.Exp,
                                       scale=-0.5))
            row_s = p.cs.val
            @block.vector
            def _(v):
                v.wait_ge(p.cs.sem, row_s)
                v.tensor_copy(rstd_b, ROWS[0:1, 1024:1536])
                p.cv.tick(v.tensor_tensor(murstd_b, mu, ROWS[0:1, 1024:1536],
                                          ALU.mult))
            row_v2 = p.cv.val
            @block.tensor
            def _(t):
                t.wait_ge(p.cv.sem, row_v2)
                t.matmul(pb[2][:], CONST["ones_row"][:], rstd_b,
                         start=True, stop=True)
                p.ct.tick(t.matmul(pb[3][:], CONST["ones_row"][:], murstd_b,
                                   start=True, stop=True))
            bc_done = p.ct.val
            ap_done = []
            for dtile in range(NDT):
                dst_ap = (xnsl(dtile, t0, TCH) if dst == "xn"
                          else xsl(dtile, t0, TCH))
                tmp = SCR[:, 1024:1536]
                @block.vector
                def _(v, dtile=dtile, dst_ap=dst_ap, tmp=tmp, tc=tc):
                    if dtile == 0:
                        v.wait_ge(p.ct.sem, bc_done)
                    if dtile == 0 and tc == 0 and wait_dX is not None:
                        v.wait_ge(p.dK.sem, wait_dX)
                    v.tensor_tensor(tmp,
                                    xsl(dtile, t0, TCH, src_shift and dtile >= 4),
                                    pb[2][:], ALU.mult)
                    v.drain()
                    if gb is None:
                        p.cv.tick(v.tensor_tensor(dst_ap, tmp, pb[3][:],
                                                  ALU.subtract))
                    else:
                        v.tensor_tensor(tmp, tmp, pb[3][:], ALU.subtract)
                        v.drain()
                        p.cv.tick(v.tensor_scalar(
                            dst_ap, tmp, gb[0][:, dtile:dtile + 1],
                            gb[1][:, dtile:dtile + 1], ALU.mult, ALU.add))
                    v.drain()
                ap_done.append(p.cv.val)
            @block.tensor
            def _(t):
                t.wait_ge(p.cv.sem, ap_done[-1])
            prev_mm_last = mm_done[-1]
        p.bar()

    WQ0, WK0, WV0, WO0 = 0, 8192, 16384, 24576
    WBETA = SCR[:, 4224:4352]
    dnUpow = [SCR[:, i * 256:(i + 1) * 256] for i in range(6)]
    dnLpow = [SCR[:, 1536 + i * 256:1536 + (i + 1) * 256] for i in range(5)]
    dnL = SCR[:, 2816:3072]
    dnU = SCR[:, 3072:3328]
    dnX0 = SCR[:, 3328:3456]
    dnX1 = SCR[:, 3456:3584]
    dnS = SCR[:, 3584:3840]
    dnKt = SCR[:, 3840:3968]
    dnKtm = SCR[:, 3968:4096]
    dnBV = SCR[:, 4096:4224]
    dnWt = SCRB[:, 0:0]
    WTSB = sb("WTSB", (128, 64), F16)
    KRB = sb("KRB", (2, 512), F16)

    def emit_dn_weights():
        @block.gpsimd
        def _(g):
            for nm, off in [("dn_wq", WQ0), ("dn_wk", WK0), ("dn_wv", WV0),
                            ("dn_wout", WO0)]:
                p.dV.tick(g.dma_start(
                    out=WAR[:, off:off + 8192],
                    in_=dt[nm].rearrange("p a b -> p (a b)")))
            p.dV.tick(g.dma_start(out=WBETA,
                                  in_=dt["dn_wbeta"].rearrange("p a b -> p (a b)")))
        return p.dV.val

    def emit_beta(dnw_val):
        if p.done:
            return
        """beta head-major (BHM) and token-major (BTM) from ln1b (XN)."""
        for tc in range(NTC):
            @block.tensor
            def _(t, tc=tc):
                if tc == 0:
                    t.wait_ge(p.dV.sem, dnw_val)
                if tc > 0:
                    t.wait_ge(p.cs.sem, p.cs.val)
                for dtile in range(NDT):
                    mm = t.matmul(pb[0][0:16, :],
                                  WBETA[:, dtile * 16:(dtile + 1) * 16],
                                  xnsl(dtile, tc * TCH, TCH),
                                  start=(dtile == 0), stop=(dtile == NDT - 1))
                    if dtile == NDT - 1:
                        p.ct.tick(mm)
            @block.scalar
            def _(s, tc=tc):
                s.wait_ge(p.ct.sem, p.ct.val)
                p.cs.tick(s.activation(BHM[0:16, tc * TCH:(tc + 1) * TCH],
                                       pb[0][0:16, :], AF.Sigmoid,
                                       bias=CONST["bbeta_col"][:]))
        for tt in range(16):
            @block.tensor
            def _(t, tt=tt):
                if tt > 0:
                    t.wait_ge(p.cv.sem, p.cv.val)
                for dtile in range(NDT):
                    mm = t.matmul(pb[1][:, 0:16],
                                  xnsl(dtile, tt * 128, 128),
                                  WBETA[:, dtile * 16:(dtile + 1) * 16],
                                  start=(dtile == 0), stop=(dtile == NDT - 1))
                    if dtile == NDT - 1:
                        p.ct.tick(mm)
            @block.vector
            def _(v, tt=tt):
                v.wait_ge(p.ct.sem, p.ct.val)
                if tt > 0:
                    v.wait_ge(p.cs.sem, p.cs.val)
                p.cv.tick(v.tensor_tensor(BTM[:, tt * 16:(tt + 1) * 16],
                                          pb[1][:, 0:16], BB[:], ALU.add))
            @block.scalar
            def _(s, tt=tt):
                s.wait_ge(p.cv.sem, p.cv.val)
                p.cs.tick(s.activation(BTM[:, tt * 16:(tt + 1) * 16],
                                       BTM[:, tt * 16:(tt + 1) * 16], AF.Sigmoid))
        p.bar()

    def emit_dn_group(g):
        if p.done:
            return
        """Projections + chunk recurrence for head group g (2 heads)."""
        qv = dn_q(); knv = dn_kn(); vtv = dn_vt()
        parts = CFGL.get("dn_proj_parts", "qvkn")
        for tc in range(NTC):
            sl = slice(tc * TCH, (tc + 1) * TCH)
            for which, wofs, dst in [("q", WQ0, qv), ("v", WV0, vtv)]:
                if which not in parts:
                    continue
                @block.tensor
                def _(t, wofs=wofs, tc=tc):
                    t.wait_ge(p.cv.sem, p.cv.val)
                    for dtile in range(NDT):
                        mm = t.matmul(
                            pb[0][:],
                            WAR[:, wofs + dtile * 1024 + g * 128:
                                wofs + dtile * 1024 + g * 128 + 128],
                            xnsl(dtile, tc * TCH, TCH),
                            start=(dtile == 0), stop=(dtile == NDT - 1))
                        if dtile == NDT - 1:
                            p.ct.tick(mm)
                @block.vector
                def _(v, dst=dst, sl=sl):
                    v.wait_ge(p.ct.sem, p.ct.val)
                    p.cv.tick(v.tensor_copy(dst[:, sl], pb[0][:]))
            if "k" not in parts:
                continue
            @block.tensor
            def _(t, tc=tc):
                t.wait_ge(p.cv.sem, p.cv.val)
                for dtile in range(NDT):
                    mm = t.matmul(
                        pb[1][:],
                        WAR[:, WK0 + dtile * 1024 + g * 128:
                            WK0 + dtile * 1024 + g * 128 + 128],
                        xnsl(dtile, tc * TCH, TCH),
                        start=(dtile == 0), stop=(dtile == NDT - 1))
                    if dtile == NDT - 1:
                        p.ct.tick(mm)
            @block.vector
            def _(v, sl=sl):
                v.wait_ge(p.ct.sem, p.ct.val)
                v.tensor_copy(knv[:, sl], pb[1][:])
                v.drain()
                p.cv.tick(v.tensor_tensor(SCR[:, 0:512], knv[:, sl],
                                          knv[:, sl], ALU.mult))
            sq_v = p.cv.val
            if "n" not in parts:
                continue
            @block.tensor
            def _(t):
                t.wait_ge(p.cv.sem, sq_v)
                p.ct.tick(t.matmul(pb[2][0:2, :], CONST["sel2"][:],
                                   SCR[:, 0:512], start=True, stop=True))
            ss_t = p.ct.val
            @block.scalar
            def _(s):
                s.wait_ge(p.ct.sem, ss_t)
                s.activation(ROWS[0:2, 0:512], pb[2][0:2, :], AF.Ln)
                s.drain()
                p.cs.tick(s.activation(ROWS[0:2, 0:512], ROWS[0:2, 0:512],
                                       AF.Exp, scale=-0.5))
            rr_s = p.cs.val
            @block.vector
            def _(v):
                v.wait_ge(p.cs.sem, rr_s)
                p.cv.tick(v.tensor_scalar_min(KRB[:], ROWS[0:2, 0:512], 60000.0))
            kr_v = p.cv.val
            @block.tensor
            def _(t):
                t.wait_ge(p.cv.sem, kr_v)
                p.ct.tick(t.matmul(pb[3][:], CONST["ind2h"][:], KRB[:],
                                   start=True, stop=True))
            bcv = p.ct.val
            @block.vector
            def _(v, sl=sl):
                v.wait_ge(p.ct.sem, bcv)
                p.cv.tick(v.tensor_tensor(knv[:, sl], knv[:, sl], pb[3][:],
                                          ALU.mult))
        for c in range(CFGL.get("dn_nchunks", NCH)):
            c0 = c * DNC
            csl = slice(c0, c0 + DNC)
            @block.tensor
            def _(t):
                t.wait_ge(p.cv.sem, p.cv.val)
                t.wait_ge(p.cs.sem, p.cs.val)
                t.matmul(pb[4][:, 0:128], CONST["head_sel"][0:16, g * 128:(g + 1) * 128],
                         BHM[0:16, csl], start=True, stop=True)
                t.transpose(pbT[:, 0:128], knv[:, csl], CONST["ident16"][:])
                p.ct.tick(t.transpose(pbT[:, 128:256], vtv[:, csl],
                                      CONST["ident16"][:]))
            tr_t = p.ct.val
            @block.vector
            def _(v, c=c):
                v.wait_ge(p.ct.sem, tr_t)
                v.tensor_tensor(dnKt, knv[:, csl], pb[4][:, 0:128], ALU.mult)
                v.tensor_copy(dnKtm, pbT[:, 0:128])
                for h in range(2):
                    lv = v.tensor_scalar_mul(
                        dnBV[:, h * 64:(h + 1) * 64],
                        pbT[:, 128 + h * 64:128 + (h + 1) * 64],
                        BTM[:, c * 16 + 2 * g + h:c * 16 + 2 * g + h + 1])
                p.cv.tick(lv)
            kt_v = p.cv.val
            if CFGL.get("dn_stage", 9) < 2:
                continue
            @block.tensor
            def _(t, c=c):
                t.wait_ge(p.cv.sem, kt_v)
                last = None
                for h in range(2):
                    hsl = slice(h * 64, (h + 1) * 64)
                    if h == 1:
                        t.drain()
                    last = t.matmul(pb[0][:, h * 128:(h + 1) * 128],
                                    dnKt[hsl, :], knv[hsl, csl],
                                    start=True, stop=True)
                    last = t.matmul(pb[0][:, 256 + h * 128:256 + (h + 1) * 128],
                                    knv[hsl, csl], dnKt[hsl, :],
                                    start=True, stop=True)
                    if c > 0:
                        last = t.matmul(pb[3][:, h * 64:(h + 1) * 64],
                                        dnKt[hsl, :], WTSB[hsl, :],
                                        start=True, stop=True)
                p.ct.tick(last)
            a_t = p.ct.val
            @block.vector
            def _(v, c=c):
                v.wait_ge(p.ct.sem, a_t)
                v.tensor_tensor(dnL, pb[0][:, 0:256], CONST["mlow"][:], ALU.mult)
                v.tensor_tensor(dnU, pb[0][:, 256:512], CONST["mup"][:], ALU.mult)
                if c > 0:
                    lv = v.tensor_tensor(dnX0, dnBV, pb[3][:, 0:128],
                                         ALU.subtract)
                else:
                    lv = v.tensor_copy(dnX0, dnBV)
                p.cv.tick(lv)
            lu_v = p.cv.val
            if CFGL.get("dn_stage", 9) < 3:
                continue
            prevU, prevL = dnU, dnL
            pw_v = lu_v
            for lvl in range(6):
                @block.tensor
                def _(t, prevU=prevU, prevL=prevL, pw_v=pw_v, lvl=lvl):
                    t.wait_ge(p.cv.sem, pw_v)
                    if lvl == 0:
                        t.drain()
                    last = None
                    for h in range(2):
                        hs2 = slice(h * 128, (h + 1) * 128)
                        last = t.matmul(pb[1][:, hs2], prevL[:, hs2],
                                        prevU[:, hs2], start=True, stop=True)
                        if lvl < 5:
                            last = t.matmul(pb[2][:, hs2], prevU[:, hs2],
                                            prevL[:, hs2], start=True, stop=True)
                    p.ct.tick(last)
                lt = p.ct.val
                @block.vector
                def _(v, lvl=lvl, lt=lt):
                    v.wait_ge(p.ct.sem, lt)
                    lv = v.tensor_copy(dnUpow[lvl], pb[1][:, 0:256])
                    if lvl < 5:
                        lv = v.tensor_copy(dnLpow[lvl], pb[2][:, 0:256])
                    p.cv.tick(lv)
                pw_v = p.cv.val
                prevU, prevL = dnUpow[lvl], dnLpow[lvl] if lvl < 5 else None
            if CFGL.get("dn_stage", 9) < 4:
                continue
            xcur = dnX0
            xnxt = dnX1
            sv_v = pw_v
            steps = [dnUpow[5], dnUpow[4], dnUpow[3], dnUpow[2], dnUpow[1],
                     dnUpow[0], dnU]
            for j, up in enumerate(steps):
                last = (j == len(steps) - 1)
                @block.tensor
                def _(t, up=up, xcur=xcur, sv_v=sv_v):
                    t.wait_ge(p.cv.sem, sv_v)
                    last = None
                    for h in range(2):
                        last = t.matmul(pb[4][:, h * 64:(h + 1) * 64],
                                        up[:, h * 128:(h + 1) * 128],
                                        xcur[:, h * 64:(h + 1) * 64],
                                        start=True, stop=True)
                    p.ct.tick(last)
                st_t = p.ct.val
                @block.vector
                def _(v, xcur=xcur, xnxt=xnxt, last=last, st_t=st_t):
                    v.wait_ge(p.ct.sem, st_t)
                    p.cv.tick(v.tensor_tensor(xnxt, xcur, pb[4][:, 0:128],
                                              ALU.subtract if last else ALU.add))
                sv_v = p.cv.val
                xcur, xnxt = xnxt, xcur
            if CFGL.get("dn_stage", 9) < 5:
                continue
            @block.tensor
            def _(t):
                t.wait_ge(p.cv.sem, p.cv.val)
                last = None
                for h in range(2):
                    hsl = slice(h * 64, (h + 1) * 64)
                    if h == 1:
                        t.drain()
                    last = t.matmul(pb[5][:, h * 128:(h + 1) * 128],
                                    knv[hsl, csl], qv[hsl, csl],
                                    start=True, stop=True)
                p.ct.tick(last)
            stt = p.ct.val
            @block.vector
            def _(v):
                v.wait_ge(p.ct.sem, stt)
                p.cv.tick(v.tensor_tensor(dnS, pb[5][:, 0:256],
                                          CONST["mupd"][:], ALU.mult))
            s_v = p.cv.val
            if CFGL.get("dn_stage", 9) < 6:
                continue
            @block.tensor
            def _(t, c=c, xcur=xcur):
                t.wait_ge(p.cv.sem, s_v)
                t.drain()
                h0, h1 = slice(0, 64), slice(64, 128)
                if c > 0:
                    t.matmul(pb[5][h0, 256:384], WTSB[h0, :],
                             qv[h0, csl], start=True, stop=False,
                             skip_group_check=True)
                    t.drain()
                    t.matmul(pb[5][h1, 256:384], WTSB[h1, :],
                             qv[h1, csl], start=True, stop=False,
                             skip_group_check=True)
                    t.drain()
                    for h, hsl in ((0, h0), (1, h1)):
                        t.matmul(pb[5][hsl, 256:384], xcur[:, hsl],
                                 dnS[:, h * 128:(h + 1) * 128],
                                 start=False, stop=True,
                                 skip_group_check=True)
                else:
                    for h, hsl in ((0, h0), (1, h1)):
                        t.matmul(pb[5][hsl, 256:384], xcur[:, hsl],
                                 dnS[:, h * 128:(h + 1) * 128],
                                 start=True, stop=True,
                                 skip_group_check=True)
                last = None
                for h, hsl in ((0, h0), (1, h1)):
                    last = t.matmul(pb[6][hsl, 0:64], dnKtm[:, hsl],
                                    xcur[:, hsl], start=(c == 0),
                                    stop=(c == NCH - 1), skip_group_check=True)
                p.ct.tick(last)
            o_t = p.ct.val
            @block.vector
            def _(v, c=c, g2=g % 2):
                v.wait_ge(p.ct.sem, o_t)
                v.tensor_copy(dn_oT(g2)[:, c * 128:(c + 1) * 128],
                              pb[5][:, 256:384])
                p.cv.tick(v.tensor_copy(WTSB[:], pb[6][:, 0:64]))

    def emit_dn_wout_pair(gpair):
        if p.done:
            return
        """X += oT(2) @ Wout rows for groups (2*gpair, 2*gpair+1)."""
        for dtile in range(NDT):
            for tc in range(NTC):
                bank = (dtile * NTC + tc) % 2
                @block.tensor
                def _(t, dtile=dtile, tc=tc, bank=bank):
                    t.wait_ge(p.cv.sem, p.cv.val)
                    for gi in range(2):
                        gg = 2 * gpair + gi
                        mm = t.matmul(
                            pb[bank][:],
                            WAR[:, WO0 + gg * 1024 + dtile * 128:
                                WO0 + gg * 1024 + dtile * 128 + 128],
                            dn_oT(gi)[:, tc * TCH:(tc + 1) * TCH],
                            start=(gi == 0), stop=(gi == 1))
                        if gi == 1:
                            p.ct.tick(mm)
                @block.vector
                def _(v, dtile=dtile, tc=tc, bank=bank):
                    v.wait_ge(p.ct.sem, p.ct.val)
                    xa = xsl(dtile, tc * TCH, TCH)
                    p.cv.tick(v.tensor_tensor(xa, xa, pb[bank][:], ALU.add))
        p.bar()

    def emit_dn(b, dnw_val):
        emit_beta(dnw_val)
        if CFGL.get("tap") == "dn_beta" and not p.done:
            dump(BHM[0:16, :], 2048, nparts=16)
            p.done = True
            return
        for g in range(CFGL.get("dn_ngroups", NG)):
            emit_dn_group(g)
            if CFGL.get("tap") == "dn_proj" and not p.done:
                dump(dn_kn(), 2048)
                p.done = True
                return
            if CFGL.get("tap") == "dn_oT" and not p.done:
                p.bar()
                dump(dn_oT(g % 2), 2048)
                p.done = True
                return
            if g % 2 == 1:
                p.bar()
                emit_dn_wout_pair(g // 2)

    SSUM = BTM[:, 0:1]
    SREC = BTM[:, 1:2]
    q_rotv = lambda ws: ASCB[:, ws * 2048:(ws + 1) * 2048]
    k_rollv = lambda s: ASCB[:, 4096 + s * 2048:4096 + (s + 1) * 2048]
    v_rollv = lambda s: ASC16[:, s * 2048:(s + 1) * 2048]
    oTv = ASC16
    hbufv = lambda ic: ASC16[:, ic * 512:(ic + 1) * 512]

    def emit_attn_weights(i):
        @block.gpsimd
        def _(g):
            g.wait_ge(p.ct.sem, p.ct.val)
            g.wait_ge(p.dV.sem, p.dV.val)
            for nm, off in [("st_wq", WQ0), ("st_wk", WK0), ("st_wv", WV0),
                            ("st_wo", WO0)]:
                p.dV.tick(g.dma_start(
                    out=WAR[:, off:off + 8192],
                    in_=dt[nm][i].rearrange("p a b -> p (a b)")))
        return p.dV.val

    def emit_layer_params(i):
        @block.sync
        def _(sy):
            sy.wait_ge(p.cv.sem, p.cv.val)
            sy.wait_ge(p.cs.sem, p.cs.val)
            p.dP.tick(sy.dma_start(out=B1A[:], in_=dt["st_b1a"][i]))
            p.dP.tick(sy.dma_start(out=B1G[:], in_=dt["st_b1g"][i]))
            p.dP.tick(sy.dma_start(out=LNG[:], in_=dt["st_lng"][i]))
            p.dP.tick(sy.dma_start(out=LNB[:], in_=dt["st_lnb"][i]))
            p.dP.tick(sy.dma_start(out=B2C[:], in_=dt["st_b2"][i]))
            p.dP.tick(sy.dma_start(out=QSC[:], in_=dt["st_qs"][i]))
            p.dP.tick(sy.dma_start(out=KSC[:], in_=dt["st_ks"][i]))
        return p.dP.val

    def emit_qk_pipeline(wofs, sc_ap, cs_tbl, ss_tbl, is_q, c, wv_val):
        if p.done:
            return
        """Project+l2norm+scale+rotary for q or k, one tchunk, all groups."""
        sl = slice(c * TCH, (c + 1) * TCH)
        q_raw = SCR[:, 0:512]
        sq = SCR[:, 512:1024]
        qn = SCRB[:, 0:512]
        tmp = SCRB[:, 512:1024]
        for g in range(NG):
            if p.done:
                return
            @block.tensor
            def _(t, g=g):
                t.wait_ge(p.cv.sem, p.cv.val)
                if g == 0:
                    t.wait_ge(p.dV.sem, wv_val)
                for dtile in range(NDT):
                    mm = t.matmul(
                        pb[0][:],
                        WAR[:, wofs + dtile * 1024 + g * 128:
                            wofs + dtile * 1024 + g * 128 + 128],
                        xsl(dtile, c * TCH, TCH, dtile >= 4),
                        start=(dtile == 0), stop=(dtile == NDT - 1))
                p.ct.tick(mm)
            @block.vector
            def _(v):
                v.wait_ge(p.ct.sem, p.ct.val)
                v.tensor_copy(q_raw, pb[0][:])
                v.drain()
                p.cv.tick(v.tensor_tensor(sq, q_raw, q_raw, ALU.mult))
            @block.tensor
            def _(t):
                t.wait_ge(p.cv.sem, p.cv.val)
                p.ct.tick(t.matmul(pb[2][0:2, :], CONST["sel2"][:], sq,
                                   start=True, stop=True))
            @block.scalar
            def _(s):
                s.wait_ge(p.ct.sem, p.ct.val)
                s.activation(ROWS[0:2, 0:512], pb[2][0:2, :], AF.Ln)
                s.drain()
                p.cs.tick(s.activation(ROWS[0:2, 0:512], ROWS[0:2, 0:512],
                                       AF.Exp, scale=-0.5))
            @block.vector
            def _(v):
                v.wait_ge(p.cs.sem, p.cs.val)
                p.cv.tick(v.tensor_scalar_min(KRB[:], ROWS[0:2, 0:512], 60000.0))
            @block.tensor
            def _(t):
                t.wait_ge(p.cv.sem, p.cv.val)
                p.ct.tick(t.matmul(pb[3][:], CONST["ind2h"][:], KRB[:],
                                   start=True, stop=True))
            @block.vector
            def _(v):
                v.wait_ge(p.ct.sem, p.ct.val)
                p.cv.tick(v.scalar_tensor_tensor(qn, q_raw, sc_ap, pb[3][:],
                                                 ALU.mult, ALU.mult))
            if CFGL.get("tap") == "qn0" and is_q and c == 0 and g == 0 and not p.done:
                dump(qn, 512)
                p.done = True
                return
            if CFGL.get("tap") == "qraw0" and is_q and c == 0 and g == 0 and not p.done:
                dump(q_raw, 512)
                p.done = True
                return
            @block.tensor
            def _(t):
                t.wait_ge(p.cv.sem, p.cv.val)
                p.ct.tick(t.matmul(pb[4][:], CONST["rsgnT"][:], qn,
                                   start=True, stop=True))
            @block.vector
            def _(v):
                v.wait_ge(p.ct.sem, p.ct.val)
                p.cv.tick(v.tensor_tensor(tmp, pb[4][:], ss_tbl, ALU.mult))
            @block.gpsimd
            def _(gp):
                gp.wait_ge(p.cv.sem, p.cv.val)
                p.cp.tick(gp.tensor_tensor(qn, qn, cs_tbl, ALU.mult))
            @block.vector
            def _(v, g=g):
                v.wait_ge(p.cp.sem, p.cp.val)
                for half in range(2):
                    w = 2 * c + half
                    dsl = (q_rotv(w % 2) if is_q
                           else k_rollv(w % 3))[:, g * 256:(g + 1) * 256]
                    lv = v.tensor_tensor(dsl, qn[:, half * 256:(half + 1) * 256],
                                         tmp[:, half * 256:(half + 1) * 256],
                                         ALU.add)
                p.cv.tick(lv)

    def emit_attn_proj(i, c, wv_val):
        if p.done:
            return
        """v/q/k projections + rotary for token chunk c."""
        @block.sync
        def _(sy):
            sy.wait_ge(p.cv.sem, p.cv.val)
            for j, nm in enumerate(("csq", "ssq", "csk", "ssk")):
                p.dX.tick(sy.dma_start(out=TBL[:, j * 512:(j + 1) * 512],
                                       in_=dt[nm][:, c * TCH:(c + 1) * TCH]))
        tbl_val = p.dX.val
        for tt in range(4):
            t0 = c * TCH + tt * 128
            b_abs = 2 * c * 2 + tt
            blk = c * 4 + tt
            slot = (blk // 2) % 3
            ttis = blk % 2
            for dvh in range(2):
                bank = (tt * 2 + dvh) % 2
                @block.tensor
                def _(t, t0=t0, dvh=dvh, bank=bank):
                    t.wait_ge(p.cv.sem, p.cv.val)
                    if c == 0 and t0 == 0 and dvh == 0:
                        t.wait_ge(p.dV.sem, wv_val)
                    last = None
                    for dtile in range(NDT):
                        last = t.matmul(pb[bank][:],
                                        xsl(dtile, t0, 128, dtile >= 4),
                                        WAR[:, WV0 + dtile * 1024 + dvh * 512:
                                            WV0 + dtile * 1024 + (dvh + 1) * 512],
                                        start=(dtile == 0),
                                        stop=(dtile == NDT - 1))
                    p.ct.tick(last)
                @block.vector
                def _(v, slot=slot, ttis=ttis, dvh=dvh, bank=bank):
                    v.wait_ge(p.ct.sem, p.ct.val)
                    p.cv.tick(v.tensor_copy(
                        v_rollv(slot)[:, ttis * 1024 + dvh * 512:
                                      ttis * 1024 + (dvh + 1) * 512],
                        pb[bank][:]))
        @block.vector
        def _(v):
            v.wait_ge(p.dX.sem, tbl_val)
        emit_qk_pipeline(WQ0, QSC[:, 0:1], TBL[:, 0:512], TBL[:, 512:1024],
                         True, c, wv_val)
        if p.done:
            return
        emit_qk_pipeline(WK0, KSC[:, 0:1], TBL[:, 1024:1536], TBL[:, 1536:2048],
                         False, c, wv_val)

    def emit_attn_window(i, w):
        if p.done:
            return
        ws = w % 2
        ncols = 256 if w == 0 else 512
        nkc = ncols // 128
        kblocks = [2 * (w - 1) + kc if w > 0 else kc for kc in range(nkc)]
        kmap = [((b // 2) % 3, b % 2) for b in kblocks]
        sa = (w - 1) % 3 if w > 0 else 0
        sbx = w % 3 if w > 0 else 0
        P_sl = [SCR[:, j * 512:(j + 1) * 512] for j in range(2)]
        PT_p = SCR[:, 2048:3072]
        NEGM = [BTM[:, 8 + j:9 + j] for j in range(2)]
        SS = [BTM[:, 12 + j:13 + j] for j in range(2)]
        SR = [BTM[:, 16 + j:17 + j] for j in range(2)]
        NR = 16
        win0_cv = p.cv.val
        sim_t = [None] * NR
        mm_v = [None] * NR
        exp_s = [None] * NR
        rc_v = [None] * NR
        nm_s = [None] * NR
        tr_t = [None] * NR
        cp_s = [None] * NR
        pv_t = [None] * NR
        ev_v = [None] * NR

        def emit_sim(r):
            h = r // 8
            g = r % 8
            hsl = slice(h * 64, (h + 1) * 64)
            bk = 2 + 2 * (r % 2)
            @block.tensor
            def _(t, r=r, h=h, g=g, hsl=hsl, bk=bk):
                if r < 2:
                    t.wait_ge(p.cv.sem, win0_cv)
                else:
                    t.wait_ge(p.cs.sem, exp_s[r - 2])
                if h == 1:
                    t.drain()
                last = None
                for qt in range(2):
                    qsl = q_rotv(ws)[hsl, g * 256 + qt * 128:
                                     g * 256 + qt * 128 + 128]
                    if w == 0:
                        last = t.matmul(pb[bk + qt][:, 0:256], qsl,
                                        k_rollv(0)[hsl, g * 256:(g + 1) * 256],
                                        start=True, stop=True)
                    else:
                        t.matmul(pb[bk + qt][:, 0:256], qsl,
                                 k_rollv(sa)[hsl, g * 256:(g + 1) * 256],
                                 start=True, stop=False)
                        last = t.matmul(pb[bk + qt][:, 256:512], qsl,
                                        k_rollv(sbx)[hsl, g * 256:(g + 1) * 256],
                                        start=False, stop=True)
                p.ct.tick(last)
            sim_t[r] = p.ct.val

        emit_sim(0)
        emit_sim(1)
        for r in range(NR):
            h = r // 8
            g = r % 8
            bk = 2 + 2 * (r % 2)
            @block.vector
            def _(v, r=r, bk=bk):
                v.wait_ge(p.ct.sem, sim_t[r])
                for qt in range(2):
                    mask = (CONST["maskB0"] if w == 0 else CONST["maskA0"]) \
                        if qt == 0 else \
                        (CONST["maskB1"] if w == 0 else CONST["maskA1"])
                    v.tensor_tensor(pb[bk + qt][:, 0:ncols],
                                    pb[bk + qt][:, 0:ncols], mask[:], ALU.add)
                v.drain()
                for qt in range(2):
                    last = v.reduce_max(NEGM[qt], pb[bk + qt][:, 0:ncols],
                                        axis=AXX, negate=True)
                p.cv.tick(last)
            mm_v[r] = p.cv.val
            @block.scalar
            def _(s, r=r, bk=bk):
                s.wait_ge(p.cv.sem, mm_v[r])
                for qt in range(2):
                    last = s.activation(P_sl[qt][:, 0:ncols],
                                        pb[bk + qt][:, 0:ncols],
                                        AF.Exp, bias=NEGM[qt], accum_out=SS[qt])
                p.cs.tick(last)
            exp_s[r] = p.cs.val
            @block.vector
            def _(v, r=r):
                v.wait_ge(p.cs.sem, exp_s[r])
                v.reciprocal(SR[0], SS[0])
                last = v.reciprocal(SR[1], SS[1])
                p.cv.tick(last)
            rc_v[r] = p.cv.val
            if r >= 2:
                @block.vector
                def _(v, r=r):
                    rr = r - 2
                    h2 = rr // 8
                    g2 = rr % 8
                    v.wait_ge(p.ct.sem, pv_t[rr])
                    p.cv.tick(v.tensor_copy(
                        oTv[h2 * 64:(h2 + 1) * 64,
                            6144 + g2 * 256:6144 + (g2 + 1) * 256],
                        pb[6][h2 * 64:(h2 + 1) * 64,
                              (rr % 2) * 256:(rr % 2) * 256 + 256]))
                ev_v[r - 2] = p.cv.val
            @block.scalar
            def _(s, r=r):
                s.wait_ge(p.cv.sem, rc_v[r])
                s.mul(P_sl[0][:, 0:ncols], P_sl[0][:, 0:ncols], SR[0])
                last = s.mul(P_sl[1][:, 0:ncols], P_sl[1][:, 0:ncols], SR[1])
                p.cs.tick(last)
            nm_s[r] = p.cs.val
            if r + 2 < NR:
                emit_sim(r + 2)
            @block.tensor
            def _(t, r=r):
                t.wait_ge(p.cs.sem, nm_s[r])
                if r >= 1:
                    t.wait_ge(p.cs.sem, cp_s[r - 1])
                if r + 2 < NR and (r + 2) // 8 == 1:
                    t.drain()
                for qt in range(2):
                    for kc in range(nkc):
                        last = t.transpose(
                            pbT[:, kc * 256 + qt * 128:kc * 256 + qt * 128 + 128],
                            P_sl[qt][:, kc * 128:(kc + 1) * 128],
                            CONST["ident16"][:])
                p.ct.tick(last)
            tr_t[r] = p.ct.val
            @block.scalar
            def _(s, r=r):
                s.wait_ge(p.ct.sem, tr_t[r])
                p.cs.tick(s.copy(PT_p[:, 0:nkc * 256], pbT[:, 0:nkc * 256]))
            cp_s[r] = p.cs.val
            @block.tensor
            def _(t, r=r, h=h, g=g):
                t.wait_ge(p.cs.sem, cp_s[r])
                if r >= 2:
                    t.wait_ge(p.cv.sem, ev_v[r - 2])
                else:
                    t.wait_ge(p.cv.sem, win0_cv)
                for kc in range(nkc):
                    slot, ttis = kmap[kc]
                    last = t.matmul(
                        pb[6][h * 64:(h + 1) * 64,
                              (r % 2) * 256:(r % 2) * 256 + 256],
                        v_rollv(slot)[:, ttis * 1024 + (2 * g + h) * 64:
                                      ttis * 1024 + (2 * g + h) * 64 + 64],
                        PT_p[:, kc * 256:(kc + 1) * 256],
                        start=(kc == 0), stop=(kc == nkc - 1),
                        skip_group_check=True)
                p.ct.tick(last)
            pv_t[r] = p.ct.val
        for rr in (NR - 2, NR - 1):
            @block.vector
            def _(v, rr=rr):
                h2 = rr // 8
                g2 = rr % 8
                v.wait_ge(p.ct.sem, pv_t[rr])
                p.cv.tick(v.tensor_copy(
                    oTv[h2 * 64:(h2 + 1) * 64,
                        6144 + g2 * 256:6144 + (g2 + 1) * 256],
                    pb[6][h2 * 64:(h2 + 1) * 64,
                          (rr % 2) * 256:(rr % 2) * 256 + 256]))
            ev_v[rr] = p.cv.val
        t0 = w * 256
        for dtile in range(NDT):
            bank = dtile % 2
            @block.tensor
            def _(t, dtile=dtile, bank=bank):
                t.wait_ge(p.cv.sem, p.cv.val)
                last = None
                for gb in range(NG):
                    last = t.matmul(pb[bank][:, 0:256],
                                    WAR[:, WO0 + gb * 1024 + dtile * 128:
                                        WO0 + gb * 1024 + dtile * 128 + 128],
                                    oTv[:, 6144 + gb * 256:6144 + (gb + 1) * 256],
                                    start=(gb == 0), stop=(gb == NG - 1))
                p.ct.tick(last)
            @block.vector
            def _(v, dtile=dtile, bank=bank):
                v.wait_ge(p.ct.sem, p.ct.val)
                xa = xsl(dtile, t0, 256)
                p.cv.tick(v.tensor_tensor(xa, xa, pb[bank][:, 0:256], ALU.add))

    def emit_attn_layer(i, wv_val, prm_val):
        @block.vector
        def _(v):
            v.wait_ge(p.dP.sem, prm_val)
        for c in range(NTC):
            if p.done:
                return
            emit_attn_proj(i, c, wv_val)
            if p.done:
                return
            p.bar()
            if CFGL.get("tap") == "qrot" and not p.done:
                dump(ASCB[:, 0:4096], 4096)
                p.done = True
                return
            if CFGL.get("tap") == "krot" and not p.done:
                dump(ASCB[:, 4096:8192], 4096)
                p.done = True
                return
            if CFGL.get("tap") == "vtm" and not p.done:
                dump(ASC16[:, 0:4096], 4096)
                p.done = True
                return
            emit_attn_window(i, 2 * c)
            p.bar()
            if p.done:
                return
            if CFGL.get("tap") == "oT0" and not p.done:
                dump(ASC16[:, 6144:8192], 2048)
                p.done = True
                return
            emit_attn_window(i, 2 * c + 1)
            p.bar()

    def emit_ffn_layer(i, prm_val):
        if p.done:
            return
        @block.vector
        def _(v):
            v.wait_ge(p.dP.sem, prm_val)
        emit_ln(1, "xn", (LNG, LNB))
        ic0 = 0
        for nchunk, icn in enumerate(FFN_CHUNKS):
            @block.gpsimd
            def _(g, ic0=ic0, icn=icn):
                g.wait_ge(p.ct.sem, p.ct.val)
                g.wait_ge(p.dV.sem, p.dV.val)
                p.dV.tick(g.dma_start(
                    out=WAR[:, 0:icn * 1024],
                    in_=dt["st_w1a"][i, :, ic0:ic0 + icn].rearrange(
                        "p a b c -> p (a b c)")))
                p.dV.tick(g.dma_start(
                    out=WAR[:, 8192:8192 + icn * 1024],
                    in_=dt["st_w1g"][i, :, ic0:ic0 + icn].rearrange(
                        "p a b c -> p (a b c)")))
                p.dV.tick(g.dma_start(
                    out=WAR[:, 16384:16384 + icn * 1024],
                    in_=dt["st_w2"][i, :, ic0:ic0 + icn].rearrange(
                        "p a b c -> p (a b c)")))
            wv = p.dV.val
            for c in range(NTC):
                sl = slice(c * TCH, (c + 1) * TCH)
                hv = [None] * icn
                gel = [None] * icn
                agt = [None] * icn
                for ici in range(icn):
                    ic = ic0 + ici
                    pa = pb[0] if ici % 2 == 0 else pb[2]
                    pg = pb[1] if ici % 2 == 0 else pb[3]
                    gsc = SCR[:, (ici % 2) * 512:(ici % 2) * 512 + 512]
                    @block.tensor
                    def _(t, ici=ici, c=c, pa=pa, pg=pg):
                        if ici < 2:
                            t.wait_ge(p.cv.sem, p.cv.val)
                            t.wait_ge(p.cs.sem, p.cs.val)
                            if ici == 0 and c == 0:
                                t.wait_ge(p.dV.sem, wv)
                        else:
                            t.wait_ge(p.cv.sem, hv[ici - 2])
                            t.wait_ge(p.cs.sem, gel[ici - 2])
                        for dtile in range(NDT):
                            t.matmul(pa[:],
                                     WAR[:, ici * 1024 + dtile * 128:
                                         ici * 1024 + dtile * 128 + 128],
                                     xnsl(dtile, c * TCH, TCH),
                                     start=(dtile == 0), stop=(dtile == NDT - 1))
                        last = None
                        for dtile in range(NDT):
                            last = t.matmul(
                                pg[:],
                                WAR[:, 8192 + ici * 1024 + dtile * 128:
                                    8192 + ici * 1024 + dtile * 128 + 128],
                                xnsl(dtile, c * TCH, TCH),
                                start=(dtile == 0), stop=(dtile == NDT - 1))
                        p.ct.tick(last)
                    agt[ici] = p.ct.val
                    @block.scalar
                    def _(s, ic=ic, ici=ici, pg=pg, gsc=gsc):
                        s.wait_ge(p.ct.sem, agt[ici])
                        if ici >= 2:
                            s.wait_ge(p.cv.sem, hv[ici - 2])
                        p.cs.tick(s.activation(gsc, pg[:], AF.Gelu,
                                               bias=B1G[:, ic:ic + 1]))
                    gel[ici] = p.cs.val
                    @block.vector
                    def _(v, ic=ic, ici=ici, pa=pa, gsc=gsc):
                        v.wait_ge(p.cs.sem, gel[ici])
                        if ici == 0:
                            v.wait_ge(p.ct.sem, p.ct.val)
                        p.cv.tick(v.scalar_tensor_tensor(
                            hbufv(ici), pa[:], B1A[:, ic:ic + 1], gsc,
                            ALU.add, ALU.mult))
                    hv[ici] = p.cv.val
                w2v = [None] * NDT
                for dtile in range(NDT):
                    bank = 4 + dtile % 2
                    @block.tensor
                    def _(t, dtile=dtile, bank=bank, icn=icn):
                        if dtile < 2:
                            t.wait_ge(p.cv.sem, hv[icn - 1])
                        else:
                            t.wait_ge(p.cv.sem, w2v[dtile - 2])
                        last = None
                        for ici in range(icn):
                            last = t.matmul(
                                pb[bank][:],
                                WAR[:, 16384 + ici * 1024 + dtile * 128:
                                    16384 + ici * 1024 + dtile * 128 + 128],
                                hbufv(ici),
                                start=(ici == 0), stop=(ici == icn - 1))
                        p.ct.tick(last)
                    w2t = p.ct.val
                    @block.vector
                    def _(v, dtile=dtile, bank=bank, last=(nchunk == 2), c=c,
                          w2t=w2t):
                        v.wait_ge(p.ct.sem, w2t)
                        xa = xsl(dtile, c * TCH, TCH)
                        if last:
                            lv = v.scalar_tensor_tensor(xa, pb[bank][:],
                                                        B2C[:, dtile:dtile + 1],
                                                        xa, ALU.add, ALU.add)
                        else:
                            lv = v.tensor_tensor(xa, xa, pb[bank][:], ALU.add)
                        p.cv.tick(lv)
                    w2v[dtile] = p.cv.val
            ic0 += icn
        p.bar()

    def emit_park(b):
        if p.done:
            return
        @block.sync
        def _(sy):
            sy.wait_ge(p.cv.sem, p.cv.val)
            sy.wait_ge(p.dW.sem, p.dW.val)
            for dtile in range(NDT):
                p.dK.tick(sy.dma_start(out=x1park[b, dtile * 128:(dtile + 1) * 128, :],
                                       in_=xsl(dtile, 0, T)))
        return p.dK.val

    def emit_final(b):
        if p.done:
            return
        for dtile in range(NDT):
            for c in range(NTC):
                @block.sync
                def _(sy, dtile=dtile, c=c):
                    sy.wait_ge(p.cv.sem, p.cv.val)
                    p.dK.tick(sy.dma_start(
                        out=X1S[:],
                        in_=x1park[b, dtile * 128:(dtile + 1) * 128,
                                   c * TCH:(c + 1) * TCH]))
                @block.vector
                def _(v, dtile=dtile, c=c):
                    v.wait_ge(p.dK.sem, p.dK.val)
                    xa = xsl(dtile, c * TCH, TCH)
                    p.cv.tick(v.tensor_tensor(xa, xa, X1S[:], ALU.add))
                @block.gpsimd
                def _(g, dtile=dtile, c=c):
                    g.wait_ge(p.cv.sem, p.cv.val)
                    p.dW.tick(g.dma_start(
                        out=xout[b, dtile * 128:(dtile + 1) * 128,
                                 c * TCH:(c + 1) * TCH],
                        in_=xsl(dtile, c * TCH, TCH)))
        @block.vector
        def _(v):
            v.wait_ge(p.dW.sem, p.dW.val)

    for b in range(CFGL["batches"]):
        if p.done:
            break
        xv = emit_load_x(b)
        dnw = emit_dn_weights()
        emit_ln(0, "xn", None, xload_val=xv)
        tap("ln1", f=lambda: dump(XN[:, 0:6656], 6656))
        if p.done:
            break
        if CFGL["do_dn"]:
            emit_dn(b, dnw)
            tap("dn_x1", f=lambda: dump(xsl(0, 0, 2048), 2048))
            if p.done:
                break
        park_val = emit_park(b)
        emit_ln(0, "x", None, wait_dX=park_val)
        tap("ln2", f=lambda: dump(xsl(0, 0, 2048), 2048))
        if p.done:
            break
        for i in range(CFGL["layers"]):
            if p.done:
                break
            wv = emit_attn_weights(i)
            prm = emit_layer_params(i)
            emit_attn_layer(i, wv, prm)
            tap(f"L{i}a", f=lambda: dump(xsl(0, 0, 2048), 2048))
            if p.done:
                break
            emit_ffn_layer(i, prm)
            tap(f"L{i}f", f=lambda: dump(xsl(0, 0, 2048), 2048))
            if p.done:
                break
        if p.done:
            break
        emit_final(b)

    es.close()
    return nc, dt



def kernel(**inputs):
    x = np.asarray(inputs["L0_attn_x"], np.float32)
    B = x.shape[0]
    xr = np.ascontiguousarray(
        x.reshape(8, 2, T, D).transpose(0, 1, 3, 2))
    w = _prep_weights(inputs)
    c = _consts()
    nc, _ = build_program(CFG)
    in_maps = []
    for core in range(8):
        m = {"xin": xr[core]}
        m.update(w)
        m.update(c)
        in_maps.append(m)
    res = run_bass_kernel_spmd(nc, in_maps, core_ids=list(range(8)))
    outs = [r["xout"] for r in res.results]
    out = np.stack(outs, 0).reshape(16, D, T).transpose(0, 2, 1)
    return np.ascontiguousarray(out)


def _debug_run(tap_name, core=0, inputs=None):
    CFG["tap"] = tap_name
    x = np.asarray(inputs["L0_attn_x"], np.float32)
    xr = np.ascontiguousarray(
        x.reshape(8, 2, T, D).transpose(0, 1, 3, 2))
    w = _prep_weights(inputs)
    c = _consts()
    nc, _ = build_program(CFG)
    m = {"xin": xr[core]}
    m.update(w)
    m.update(c)
    res = run_bass_kernel_spmd(nc, [m], core_ids=[0])
    CFG["tap"] = None
    return res.results[0]


# version 09 (final)
# speedup vs baseline: 1.3143x; 1.3143x over previous
import math
from contextlib import ExitStack

import numpy as np
import ml_dtypes

import concourse.bass as bass
import concourse.mybir as mybir
from concourse.bass_utils import run_bass_kernel_spmd

F32 = mybir.dt.float32
F16 = mybir.dt.float16
BF16 = mybir.dt.bfloat16
AF = mybir.ActivationFunctionType
ALU = mybir.AluOpType
AXX = mybir.AxisListType.X

D = 1024
T = 2048
NDT = 8
NG = 8
H = 16
DH = 64
TCH = 512
NTC = 4
DNC = 128
NCH = 16
PADI = 2816
NIC = 22
FFN_CHUNKS = (8, 8, 6)
WIN = 256
NW = 8
NEG = -1e30

CFG = {
    "tap": None,
    "layers": 6,
    "do_dn": True,
    "batches": 2,
}



def _rotary_tables():
    half = np.arange(0, DH, 2, dtype=np.float64)
    inv_freq = 1.0 / 10000.0 ** (half / DH)
    t = np.arange(T, dtype=np.float64)
    freqs = t[:, None] * inv_freq[None, :]
    freqs = np.concatenate([freqs, freqs], -1)
    base = (half + 0.4 * DH) / (1.4 * DH)
    power = (t - T // 2) / (WIN // 2)
    scale = base[None, :] ** power[:, None]
    scale = np.concatenate([scale, scale], -1)
    cs_q = (np.cos(freqs) * scale * 8.0).T
    ss_q = (np.sin(freqs) * scale * 8.0).T
    cs_k = (np.cos(freqs) / scale).T
    ss_k = (np.sin(freqs) / scale).T
    out = {}
    for nm, a in [("csq", cs_q), ("ssq", ss_q), ("csk", cs_k), ("ssk", ss_k)]:
        out[nm] = np.ascontiguousarray(
            np.tile(a, (2, 1)).astype(ml_dtypes.bfloat16))
    return out


def _consts():
    c = {}
    c.update(_rotary_tables())
    c["ident16"] = np.eye(128, dtype=np.float16)
    c["identbf"] = np.eye(128, dtype=ml_dtypes.bfloat16)
    c["ones_col"] = np.ones((128, 1), dtype=np.float16)
    c["ones_row"] = np.ones((1, 128), dtype=np.float16)
    c["ones_rowb"] = np.ones((1, 128), dtype=ml_dtypes.bfloat16)
    c["ones_row32"] = np.ones((1, 128), dtype=np.float32)
    s2 = np.zeros((128, 2), dtype=np.float16)
    s2[0:64, 0] = 1; s2[64:128, 1] = 1
    c["sel2"] = s2
    i2 = np.zeros((2, 128), dtype=ml_dtypes.bfloat16)
    i2[0, 0:64] = 1; i2[1, 64:128] = 1
    c["ind2b"] = i2
    c["ind2h"] = i2.astype(np.float16)
    hs = np.zeros((16, 1024), dtype=np.float16)
    for i in range(1024):
        hs[i // 64, i] = 1
    c["head_sel"] = hs
    R = np.zeros((64, 64), dtype=np.float64)
    for i in range(32):
        R[i, i + 32] = -1.0
        R[i + 32, i] = 1.0
    RT = R.T
    rb = np.zeros((128, 128), dtype=np.float64)
    rb[0:64, 0:64] = RT; rb[64:128, 64:128] = RT
    c["rsgnT"] = rb.astype(ml_dtypes.bfloat16)
    tl = np.tril(np.ones((128, 128)), -1)
    tu = np.triu(np.ones((128, 128)), 1)
    tud = np.triu(np.ones((128, 128)), 0)
    c["mlow"] = np.concatenate([tl, tl], 1).astype(np.float16)
    c["mup"] = np.concatenate([tu, tu], 1).astype(np.float16)
    c["mupd"] = np.concatenate([tud, tud], 1).astype(np.float16)
    def amask(qi, ncols, kofs):
        m = np.zeros((128, ncols), dtype=np.float64)
        for r in range(128):
            i = 128 * qi + r + 256
            j = np.arange(ncols) + kofs
            m[r] = np.where((i - j < 0) | (i - j > 256), NEG, 0.0)
        return m
    c["maskA0"] = amask(0, 512, 0).astype(ml_dtypes.bfloat16)
    c["maskA1"] = amask(1, 512, 0).astype(ml_dtypes.bfloat16)
    def amask0(qi):
        m = np.zeros((128, 256), dtype=np.float64)
        for r in range(128):
            i = 128 * qi + r
            j = np.arange(256)
            m[r] = np.where((i - j < 0) | (i - j > 256), NEG, 0.0)
        return m
    c["maskB0"] = amask0(0).astype(ml_dtypes.bfloat16)
    c["maskB1"] = amask0(1).astype(ml_dtypes.bfloat16)
    c["fvals"] = np.tile(np.array([[1e-5, -0.5, 1.0 / 1024.0, 1e12]],
                                  dtype=np.float32), (128, 1))
    return c


def _prep_weights(inp):
    w = {}
    t128 = lambda a: np.ascontiguousarray(
        a.reshape(8, 128, a.shape[-1]).transpose(1, 0, 2))
    w["dn_wq"] = t128(np.asarray(inp["L0_attn_Wq"], np.float32))
    w["dn_wk"] = t128(np.asarray(inp["L0_attn_Wk"], np.float32))
    w["dn_wv"] = t128(np.asarray(inp["L0_attn_Wv"], np.float32))
    w["dn_wout"] = t128(np.asarray(inp["L0_attn_Wout"], np.float32))
    w["dn_wbeta"] = t128(np.asarray(inp["L0_attn_Wbeta"], np.float32))
    bb = np.asarray(inp["L0_attn_b_beta"], np.float32)
    w["dn_bbeta_col"] = np.ascontiguousarray(bb.reshape(16, 1))
    w["dn_bbeta_row"] = np.ascontiguousarray(bb.reshape(1, 16))

    wqkv = np.asarray(inp["L0_ffn_Wqkv"], np.float32)
    t128L = lambda a: np.ascontiguousarray(
        a.reshape(6, 8, 128, a.shape[-1]).transpose(0, 2, 1, 3))
    w["st_wq"] = t128L(wqkv[:, :, 0:1024])
    w["st_wk"] = t128L(wqkv[:, :, 1024:2048])
    w["st_wv"] = t128L(wqkv[:, :, 2048:3072])
    w["st_wo"] = t128L(np.asarray(inp["L0_ffn_Wo"], np.float32))
    W1 = np.asarray(inp["L0_ffn_W1"], np.float32)
    DI = 2730
    W1a = np.zeros((6, 1024, PADI), np.float32); W1a[:, :, :DI] = W1[:, :, :DI]
    W1g = np.zeros((6, 1024, PADI), np.float32); W1g[:, :, :DI] = W1[:, :, DI:]
    tW1 = lambda a: np.ascontiguousarray(
        a.reshape(6, 8, 128, 22, 128).transpose(0, 2, 3, 1, 4))
    w["st_w1a"] = tW1(W1a)
    w["st_w1g"] = tW1(W1g)
    W2 = np.asarray(inp["L0_ffn_W2"], np.float32)
    W2p = np.zeros((6, PADI, 1024), np.float32); W2p[:, :DI] = W2
    w["st_w2"] = np.ascontiguousarray(
        W2p.reshape(6, 22, 128, 8, 128).transpose(0, 2, 1, 3, 4))
    b1 = np.asarray(inp["L0_ffn_b1"], np.float32)
    b1a = np.zeros((6, PADI), np.float32); b1a[:, :DI] = b1[:, :DI]
    b1g = np.zeros((6, PADI), np.float32); b1g[:, :DI] = b1[:, DI:]
    w["st_b1a"] = np.ascontiguousarray(b1a.reshape(6, 22, 128).transpose(0, 2, 1))
    w["st_b1g"] = np.ascontiguousarray(b1g.reshape(6, 22, 128).transpose(0, 2, 1))
    col8 = lambda a: np.ascontiguousarray(a.reshape(6, 8, 128).transpose(0, 2, 1))
    w["st_lng"] = col8(np.asarray(inp["L0_ffn_ln_g"], np.float32))
    w["st_lnb"] = col8(np.asarray(inp["L0_ffn_ln_b"], np.float32))
    w["st_b2"] = col8(np.asarray(inp["L0_ffn_b2"], np.float32))
    qs = np.asarray(inp["L0_ffn_q_scale"], np.float32)
    ks = np.asarray(inp["L0_ffn_k_scale"], np.float32)
    w["st_qs"] = np.ascontiguousarray(np.tile(qs, (1, 2)).reshape(6, 128, 1))
    w["st_ks"] = np.ascontiguousarray(np.tile(ks, (1, 2)).reshape(6, 128, 1))
    return w



class Ch:
    def __init__(self, es, nc, name, step):
        self.sem = es.enter_context(nc.semaphore(name))
        self.step = step
        self.n = 0
    @property
    def val(self):
        return self.n * self.step
    def tick(self, instr):
        instr.then_inc(self.sem, self.step)
        self.n += 1
        return self.val


class P:
    def __init__(self, nc, block, es):
        self.nc = nc
        self.block = block
        self.es = es
        self.ct = Ch(es, nc, "ct", 1)
        self.cv = Ch(es, nc, "cv", 1)
        self.cs = Ch(es, nc, "cs", 1)
        self.cp = Ch(es, nc, "cp", 1)
        self.dW = Ch(es, nc, "dW", 16)
        self.dV = Ch(es, nc, "dV", 16)
        self.dX = Ch(es, nc, "dX", 16)
        self.dP = Ch(es, nc, "dP", 16)
        self.dK = Ch(es, nc, "dK", 16)
        self.done = False

    def bar(self):
        if self.done:
            return
        b = self.block
        vals = {}
        chs = [("tensor", self.ct), ("vector", self.cv),
               ("scalar", self.cs), ("gpsimd", self.cp)]
        for name, ch in chs:
            def mk(ch):
                def f(e):
                    ch.tick(e.drain())
                return f
            getattr(b, name)(mk(ch))
            vals[name] = ch.val
        for name, ch in chs:
            def mk2(vals, me):
                def f(e):
                    for nm2, ch2 in chs:
                        if nm2 != me:
                            e.wait_ge(ch2.sem, vals[nm2])
                return f
            getattr(b, name)(mk2(vals, name))


def _act_set_idx(nc, funcs):
    from concourse.hw_specs import get_activation_tables
    for i, fs in enumerate(get_activation_tables(nc.m.arch).values()):
        if all(f in fs for f in funcs):
            return i
    raise ValueError(f"no act set with {funcs}")


def build_program(CFGL):
    nc = bass.Bass("TRN2", target_bir_lowering=False)
    LNEXP_SET = _act_set_idx(nc, (AF.Ln, AF.Exp, AF.Copy))

    def preload_lnexp(s):
        s.add_instruction(mybir.InstLoadActFuncSet(
            name=nc.get_next_instruction_name(),
            act_func_set_id=LNEXP_SET, ins=[], outs=[]))
    es = ExitStack()

    dt = {}
    def din(name, shape, dty=F32):
        dt[name] = nc.dram_tensor(name, list(shape), dty, kind="ExternalInput").ap()
    din("xin", (2, D, T))
    din("dn_wq", (128, 8, 1024)); din("dn_wk", (128, 8, 1024))
    din("dn_wv", (128, 8, 1024)); din("dn_wout", (128, 8, 1024))
    din("dn_wbeta", (128, 8, 16))
    din("dn_bbeta_col", (16, 1)); din("dn_bbeta_row", (1, 16))
    din("st_wq", (6, 128, 8, 1024)); din("st_wk", (6, 128, 8, 1024))
    din("st_wv", (6, 128, 8, 1024)); din("st_wo", (6, 128, 8, 1024))
    din("st_w1a", (6, 128, 22, 8, 128)); din("st_w1g", (6, 128, 22, 8, 128))
    din("st_w2", (6, 128, 22, 8, 128))
    din("st_b1a", (6, 128, 22)); din("st_b1g", (6, 128, 22))
    din("st_lng", (6, 128, 8)); din("st_lnb", (6, 128, 8)); din("st_b2", (6, 128, 8))
    din("st_qs", (6, 128, 1)); din("st_ks", (6, 128, 1))
    din("csq", (128, T), BF16); din("ssq", (128, T), BF16)
    din("csk", (128, T), BF16); din("ssk", (128, T), BF16)
    din("ident16", (128, 128), F16); din("identbf", (128, 128), BF16)
    din("ones_col", (128, 1), F16); din("ones_row", (1, 128), F16)
    din("ones_rowb", (1, 128), BF16)
    din("ones_row32", (1, 128), F32)
    din("sel2", (128, 2), F16); din("ind2b", (2, 128), BF16)
    din("ind2h", (2, 128), F16)
    din("head_sel", (16, 1024), F16)
    din("rsgnT", (128, 128), BF16)
    din("mlow", (128, 256), F16); din("mup", (128, 256), F16)
    din("mupd", (128, 256), F16)
    din("maskA0", (128, 512), BF16); din("maskA1", (128, 512), BF16)
    din("maskB0", (128, 256), BF16); din("maskB1", (128, 256), BF16)
    din("fvals", (128, 4), F32)

    xout = nc.dram_tensor("xout", [2, D, T], F32, kind="ExternalOutput").ap()
    dbg = nc.dram_tensor("dbg", [128, 16448], F32, kind="ExternalOutput").ap()
    x1park = nc.dram_tensor("x1park", [2, D, T], F16, kind="Internal").ap()

    def sb(name, shape, dty):
        return es.enter_context(nc.sbuf_tensor(name, list(shape), dty))
    X = sb("X", (128, 8 * (T + 1)), F16)
    XN = sb("XN", (128, 8 * T), F16)
    ASC16 = sb("ASC16", (128, 10240), F16)
    ASCB = sb("ASCB", (128, 10240), BF16)
    WAR = sb("WAR", (128, 32768), F16)
    SCR = sb("SCR", (128, 4352), F16)
    SCRB = sb("SCRB", (128, 1024), BF16)
    TBL = sb("TBL", (128, 4 * 512), BF16)
    ROWS = sb("ROWS", (2, 1536), F32)
    ROWB = sb("ROWB", (1, 1024), F16)
    BHM = sb("BHM", (16, T), F16)
    BTM = sb("BTM", (128, 256), F32)
    BB = sb("BB", (128, 16), F32)
    X1S = sb("X1S", (128, 512), F16)
    CONST = {}
    cshapes = {
        "ident16": ((128, 128), F16), "identbf": ((128, 128), BF16),
        "ones_col": ((128, 1), F16), "ones_row": ((1, 128), F16),
        "ones_rowb": ((1, 128), BF16), "ones_row32": ((1, 128), F32),
        "sel2": ((128, 2), F16), "ind2b": ((2, 128), BF16),
        "ind2h": ((2, 128), F16),
        "head_sel": ((16, 1024), F16), "rsgnT": ((128, 128), BF16),
        "mlow": ((128, 256), F16), "mup": ((128, 256), F16),
        "mupd": ((128, 256), F16),
        "maskA0": ((128, 512), BF16), "maskA1": ((128, 512), BF16),
        "maskB0": ((128, 256), BF16), "maskB1": ((128, 256), BF16),
        "bbeta_col": ((16, 1), F32), "bbeta_row": ((1, 16), F32),
        "fvals": ((128, 4), F32),
    }
    for nm, (shp, dty) in cshapes.items():
        CONST[nm] = sb("c_" + nm, shp, dty)
    B1A = sb("B1A", (128, 22), F32); B1G = sb("B1G", (128, 22), F32)
    LNG = sb("LNG", (128, 8), F32); LNB = sb("LNB", (128, 8), F32)
    B2C = sb("B2C", (128, 8), F32)
    QSC = sb("QSC", (128, 1), F32); KSC = sb("KSC", (128, 1), F32)

    pb = [es.enter_context(nc.psum_tensor(f"pb{i}", [128, 512], F32))
          for i in range(7)]
    pbT = es.enter_context(nc.psum_tensor("pbT", [128, 1024], F16))

    def xsl(dtile, t0, n, shift=0):
        base = dtile * (T + 1) + 1 + t0 - shift
        return X[:, base:base + n]

    def xnsl(dtile, t0, n):
        return XN[:, dtile * T + t0:dtile * T + t0 + n]

    v_roll = lambda s: ASC16[:, s * 2048:(s + 1) * 2048]
    oT_sl = lambda: ASC16[:, 6144:8192]
    dn_q = lambda: ASC16[:, 0:2048]
    dn_kn = lambda: ASC16[:, 2048:4096]
    dn_vt = lambda: ASC16[:, 4096:6144]
    dn_oT = lambda g2: ASC16[:, 6144 + g2 * 2048: 8192 + g2 * 2048]
    ffn_h = lambda: ASC16[:, 0:8 * 512]
    q_rot = lambda s: ASCB[:, s * 2048:(s + 1) * 2048]
    k_roll = lambda s: ASCB[:, 4096 + s * 2048: 6144 + s * 2048]
    war = lambda o, n: WAR[:, o:o + n]

    block = es.enter_context(nc.Block())
    p = P(nc, block, es)

    taps = []

    def tap(name, ap_f32src=None, f=None):
        if CFGL["tap"] == name and not p.done:
            f()
            p.done = True

    def dump(src_ap, cols, nparts=128):
        stage = SCR[:, 0:512]
        nchunks = (cols + 511) // 512
        for i in range(nchunks):
            n = min(512, cols - i * 512)
            @block.vector
            def _(v, i=i, n=n):
                v.wait_ge(p.ct.sem, p.ct.val)
                v.wait_ge(p.cs.sem, p.cs.val)
                v.wait_ge(p.dW.sem, p.dW.val)
                v.wait_ge(p.cv.sem, p.cv.val)
                v.drain()
                p.cv.tick(v.tensor_copy(stage[0:nparts, 0:n],
                                        src_ap[:, i * 512:i * 512 + n]))
            @block.gpsimd
            def _(g, i=i, n=n):
                g.wait_ge(p.cv.sem, p.cv.val)
                p.dW.tick(g.dma_start(out=dbg[0:nparts, i * 512:i * 512 + n],
                                      in_=stage[0:nparts, 0:n]))
            @block.vector
            def _(v):
                v.wait_ge(p.dW.sem, p.dW.val)

    @block.sync
    def _(sy):
        for nm in cshapes:
            src = dt["dn_" + nm] if nm.startswith("bbeta") else dt[nm]
            p.dX.tick(sy.dma_start(out=CONST[nm][:], in_=src[:]))
    consts_done = p.dX.val
    for i, v in enumerate([1e-5, -0.5, 1.0 / 1024.0, 1e12]):
        nc.const_aps.register(F32, v, CONST["fvals"][:, i:i + 1])
    for engn in ("tensor", "vector", "scalar", "gpsimd"):
        def mkw(engn=engn):
            def f(e):
                e.wait_ge(p.dX.sem, consts_done)
            return f
        getattr(block, engn)(mkw())

    @block.vector
    def _(v):
        v.memset(X[:], 0.0)
        p.cv.tick(v.memset(BTM[:], 0.0))
    xzero_done = p.cv.val

    @block.tensor
    def _(t):
        t.wait_ge(p.dX.sem, consts_done)
        p.ct.tick(t.matmul(pb[0][:, 0:16], CONST["ones_row32"][:],
                           CONST["bbeta_row"][:], start=True, stop=True))
    @block.scalar
    def _(s):
        s.wait_ge(p.ct.sem, p.ct.val)
        p.cs.tick(s.copy(BB[:], pb[0][:, 0:16]))
    bb_done = p.cs.val
    p.bar()

    def emit_load_x(b):
        @block.gpsimd
        def _(g):
            g.wait_ge(p.cv.sem, max(xzero_done, p.cv.val))
            g.wait_ge(p.dW.sem, p.dW.val)
            for dtile in range(NDT):
                p.dW.tick(g.dma_start(out=xsl(dtile, 0, T),
                                      in_=dt["xin"][b, dtile * 128:(dtile + 1) * 128, :]))
        return p.dW.val

    def emit_ln(src_shift, dst, gb, xload_val=None, wait_dX=None):
        sq = SCR[:, 0:512]
        sq2 = SCR[:, 512:1024]
        prev_mm_last = None
        tcs = CFGL.get("ln_tcs", range(NTC))
        for tc in tcs:
            t0 = tc * TCH
            sq_done = []
            for dtile in range(NDT):
                buf = sq if dtile % 2 == 0 else sq2
                @block.vector
                def _(v, dtile=dtile, buf=buf, tc=tc):
                    if xload_val is not None and tc == 0 and dtile == 0:
                        v.wait_ge(p.dW.sem, xload_val)
                    if dtile >= 2:
                        v.wait_ge(p.ct.sem, mm_done[dtile - 2])
                    elif prev_mm_last is not None:
                        v.wait_ge(p.ct.sem, prev_mm_last)
                    xa = xsl(dtile, t0, TCH, src_shift and dtile >= 4)
                    p.cv.tick(v.tensor_tensor(buf, xa, xa, ALU.mult))
                sq_done.append(p.cv.val)
                @block.tensor
                def _(t, dtile=dtile, buf=buf):
                    if dtile == 0 and xload_val is not None and tc == 0:
                        t.wait_ge(p.dW.sem, xload_val)
                    t.matmul(pb[0][0:1, :], CONST["ones_col"][:],
                             xsl(dtile, t0, TCH, src_shift and dtile >= 4),
                             start=(dtile == 0), stop=(dtile == NDT - 1))
                    t.wait_ge(p.cv.sem, sq_done[dtile])
                    p.ct.tick(t.matmul(pb[1][0:1, :], CONST["ones_col"][:], buf,
                             start=(dtile == 0), stop=(dtile == NDT - 1)))
                if dtile == 0:
                    mm_done = []
                mm_done.append(p.ct.val)
            mu = ROWS[0:1, 0:512]
            var = ROWS[0:1, 512:1024]
            rstd_b = ROWB[0:1, 0:512]
            murstd_b = ROWB[0:1, 512:1024]
            @block.vector
            def _(v):
                v.wait_ge(p.ct.sem, mm_done[-1])
                v.tensor_scalar_mul(mu, pb[0][0:1, :], 1.0 / 1024.0)
                v.tensor_scalar_mul(var, pb[1][0:1, :], 1.0 / 1024.0)
                v.drain()
                v.tensor_tensor(ROWS[0:1, 1024:1536], mu, mu, ALU.mult)
                v.drain()
                p.cv.tick(v.tensor_tensor(var, var, ROWS[0:1, 1024:1536],
                                          ALU.subtract))
            row_v = p.cv.val
            @block.scalar
            def _(s):
                s.wait_ge(p.cv.sem, row_v)
                s.activation(var, var, AF.Ln, bias=1e-5)
                s.drain()
                p.cs.tick(s.activation(ROWS[0:1, 1024:1536], var, AF.Exp,
                                       scale=-0.5))
            row_s = p.cs.val
            @block.vector
            def _(v):
                v.wait_ge(p.cs.sem, row_s)
                v.tensor_copy(rstd_b, ROWS[0:1, 1024:1536])
                p.cv.tick(v.tensor_tensor(murstd_b, mu, ROWS[0:1, 1024:1536],
                                          ALU.mult))
            row_v2 = p.cv.val
            @block.tensor
            def _(t):
                t.wait_ge(p.cv.sem, row_v2)
                t.matmul(pb[2][:], CONST["ones_row"][:], rstd_b,
                         start=True, stop=True)
                p.ct.tick(t.matmul(pb[3][:], CONST["ones_row"][:], murstd_b,
                                   start=True, stop=True))
            bc_done = p.ct.val
            ap_done = []
            for dtile in range(NDT):
                dst_ap = (xnsl(dtile, t0, TCH) if dst == "xn"
                          else xsl(dtile, t0, TCH))
                tmp = SCR[:, 1024:1536]
                @block.vector
                def _(v, dtile=dtile, dst_ap=dst_ap, tmp=tmp, tc=tc):
                    if dtile == 0:
                        v.wait_ge(p.ct.sem, bc_done)
                    if dtile == 0 and tc == 0 and wait_dX is not None:
                        v.wait_ge(p.dK.sem, wait_dX)
                    v.tensor_tensor(tmp,
                                    xsl(dtile, t0, TCH, src_shift and dtile >= 4),
                                    pb[2][:], ALU.mult)
                    v.drain()
                    if gb is None:
                        p.cv.tick(v.tensor_tensor(dst_ap, tmp, pb[3][:],
                                                  ALU.subtract))
                    else:
                        v.tensor_tensor(tmp, tmp, pb[3][:], ALU.subtract)
                        v.drain()
                        p.cv.tick(v.tensor_scalar(
                            dst_ap, tmp, gb[0][:, dtile:dtile + 1],
                            gb[1][:, dtile:dtile + 1], ALU.mult, ALU.add))
                    v.drain()
                ap_done.append(p.cv.val)
            @block.tensor
            def _(t):
                t.wait_ge(p.cv.sem, ap_done[-1])
            prev_mm_last = mm_done[-1]
        p.bar()

    WQ0, WK0, WV0, WO0 = 0, 8192, 16384, 24576
    WBETA = SCR[:, 4224:4352]
    dnUpow = [SCR[:, i * 256:(i + 1) * 256] for i in range(6)]
    dnLpow = [SCR[:, 1536 + i * 256:1536 + (i + 1) * 256] for i in range(5)]
    dnL = SCR[:, 2816:3072]
    dnU = SCR[:, 3072:3328]
    dnX0 = SCR[:, 3328:3456]
    dnX1 = SCR[:, 3456:3584]
    dnS = SCR[:, 3584:3840]
    dnKt = SCR[:, 3840:3968]
    dnKtm = SCR[:, 3968:4096]
    dnBV = SCR[:, 4096:4224]
    dnWt = SCRB[:, 0:0]
    WTSB = sb("WTSB", (128, 64), F16)
    KRB = sb("KRB", (2, 512), F16)

    def emit_dn_weights():
        @block.gpsimd
        def _(g):
            for nm, off in [("dn_wq", WQ0), ("dn_wk", WK0), ("dn_wv", WV0),
                            ("dn_wout", WO0)]:
                p.dV.tick(g.dma_start(
                    out=WAR[:, off:off + 8192],
                    in_=dt[nm].rearrange("p a b -> p (a b)")))
            p.dV.tick(g.dma_start(out=WBETA,
                                  in_=dt["dn_wbeta"].rearrange("p a b -> p (a b)")))
        return p.dV.val

    def emit_beta(dnw_val):
        if p.done:
            return
        """beta head-major (BHM) and token-major (BTM) from ln1b (XN)."""
        for tc in range(NTC):
            @block.tensor
            def _(t, tc=tc):
                if tc == 0:
                    t.wait_ge(p.dV.sem, dnw_val)
                if tc > 0:
                    t.wait_ge(p.cs.sem, p.cs.val)
                for dtile in range(NDT):
                    mm = t.matmul(pb[0][0:16, :],
                                  WBETA[:, dtile * 16:(dtile + 1) * 16],
                                  xnsl(dtile, tc * TCH, TCH),
                                  start=(dtile == 0), stop=(dtile == NDT - 1))
                    if dtile == NDT - 1:
                        p.ct.tick(mm)
            @block.scalar
            def _(s, tc=tc):
                s.wait_ge(p.ct.sem, p.ct.val)
                p.cs.tick(s.activation(BHM[0:16, tc * TCH:(tc + 1) * TCH],
                                       pb[0][0:16, :], AF.Sigmoid,
                                       bias=CONST["bbeta_col"][:]))
        for tt in range(16):
            @block.tensor
            def _(t, tt=tt):
                if tt > 0:
                    t.wait_ge(p.cv.sem, p.cv.val)
                for dtile in range(NDT):
                    mm = t.matmul(pb[1][:, 0:16],
                                  xnsl(dtile, tt * 128, 128),
                                  WBETA[:, dtile * 16:(dtile + 1) * 16],
                                  start=(dtile == 0), stop=(dtile == NDT - 1))
                    if dtile == NDT - 1:
                        p.ct.tick(mm)
            @block.vector
            def _(v, tt=tt):
                v.wait_ge(p.ct.sem, p.ct.val)
                if tt > 0:
                    v.wait_ge(p.cs.sem, p.cs.val)
                p.cv.tick(v.tensor_tensor(BTM[:, tt * 16:(tt + 1) * 16],
                                          pb[1][:, 0:16], BB[:], ALU.add))
            @block.scalar
            def _(s, tt=tt):
                s.wait_ge(p.cv.sem, p.cv.val)
                p.cs.tick(s.activation(BTM[:, tt * 16:(tt + 1) * 16],
                                       BTM[:, tt * 16:(tt + 1) * 16], AF.Sigmoid))
        p.bar()

    def emit_dn_group(g):
        if p.done:
            return
        """Projections + chunk recurrence for head group g (2 heads)."""
        qv = dn_q(); knv = dn_kn(); vtv = dn_vt()
        parts = CFGL.get("dn_proj_parts", "qvkn")
        for tc in range(NTC):
            sl = slice(tc * TCH, (tc + 1) * TCH)
            for which, wofs, dst in [("q", WQ0, qv), ("v", WV0, vtv)]:
                if which not in parts:
                    continue
                @block.tensor
                def _(t, wofs=wofs, tc=tc):
                    t.wait_ge(p.cv.sem, p.cv.val)
                    for dtile in range(NDT):
                        mm = t.matmul(
                            pb[0][:],
                            WAR[:, wofs + dtile * 1024 + g * 128:
                                wofs + dtile * 1024 + g * 128 + 128],
                            xnsl(dtile, tc * TCH, TCH),
                            start=(dtile == 0), stop=(dtile == NDT - 1))
                        if dtile == NDT - 1:
                            p.ct.tick(mm)
                @block.vector
                def _(v, dst=dst, sl=sl):
                    v.wait_ge(p.ct.sem, p.ct.val)
                    p.cv.tick(v.tensor_copy(dst[:, sl], pb[0][:]))
            if "k" not in parts:
                continue
            @block.tensor
            def _(t, tc=tc):
                t.wait_ge(p.cv.sem, p.cv.val)
                for dtile in range(NDT):
                    mm = t.matmul(
                        pb[1][:],
                        WAR[:, WK0 + dtile * 1024 + g * 128:
                            WK0 + dtile * 1024 + g * 128 + 128],
                        xnsl(dtile, tc * TCH, TCH),
                        start=(dtile == 0), stop=(dtile == NDT - 1))
                    if dtile == NDT - 1:
                        p.ct.tick(mm)
            @block.vector
            def _(v, sl=sl):
                v.wait_ge(p.ct.sem, p.ct.val)
                v.tensor_copy(knv[:, sl], pb[1][:])
                v.drain()
                p.cv.tick(v.tensor_tensor(SCR[:, 0:512], knv[:, sl],
                                          knv[:, sl], ALU.mult))
            sq_v = p.cv.val
            if "n" not in parts:
                continue
            @block.tensor
            def _(t):
                t.wait_ge(p.cv.sem, sq_v)
                p.ct.tick(t.matmul(pb[2][0:2, :], CONST["sel2"][:],
                                   SCR[:, 0:512], start=True, stop=True))
            ss_t = p.ct.val
            @block.scalar
            def _(s):
                s.wait_ge(p.ct.sem, ss_t)
                s.activation(ROWS[0:2, 0:512], pb[2][0:2, :], AF.Ln)
                s.drain()
                p.cs.tick(s.activation(ROWS[0:2, 0:512], ROWS[0:2, 0:512],
                                       AF.Exp, scale=-0.5))
            rr_s = p.cs.val
            @block.vector
            def _(v):
                v.wait_ge(p.cs.sem, rr_s)
                p.cv.tick(v.tensor_scalar_min(KRB[:], ROWS[0:2, 0:512], 60000.0))
            kr_v = p.cv.val
            @block.tensor
            def _(t):
                t.wait_ge(p.cv.sem, kr_v)
                p.ct.tick(t.matmul(pb[3][:], CONST["ind2h"][:], KRB[:],
                                   start=True, stop=True))
            bcv = p.ct.val
            @block.vector
            def _(v, sl=sl):
                v.wait_ge(p.ct.sem, bcv)
                p.cv.tick(v.tensor_tensor(knv[:, sl], knv[:, sl], pb[3][:],
                                          ALU.mult))
        for c in range(CFGL.get("dn_nchunks", NCH)):
            c0 = c * DNC
            csl = slice(c0, c0 + DNC)
            @block.tensor
            def _(t):
                t.wait_ge(p.cv.sem, p.cv.val)
                t.wait_ge(p.cs.sem, p.cs.val)
                t.matmul(pb[4][:, 0:128], CONST["head_sel"][0:16, g * 128:(g + 1) * 128],
                         BHM[0:16, csl], start=True, stop=True)
                t.transpose(pbT[:, 0:128], knv[:, csl], CONST["ident16"][:])
                p.ct.tick(t.transpose(pbT[:, 128:256], vtv[:, csl],
                                      CONST["ident16"][:]))
            tr_t = p.ct.val
            @block.vector
            def _(v, c=c):
                v.wait_ge(p.ct.sem, tr_t)
                v.tensor_tensor(dnKt, knv[:, csl], pb[4][:, 0:128], ALU.mult)
                v.tensor_copy(dnKtm, pbT[:, 0:128])
                for h in range(2):
                    lv = v.tensor_scalar_mul(
                        dnBV[:, h * 64:(h + 1) * 64],
                        pbT[:, 128 + h * 64:128 + (h + 1) * 64],
                        BTM[:, c * 16 + 2 * g + h:c * 16 + 2 * g + h + 1])
                p.cv.tick(lv)
            kt_v = p.cv.val
            if CFGL.get("dn_stage", 9) < 2:
                continue
            @block.tensor
            def _(t, c=c):
                t.wait_ge(p.cv.sem, kt_v)
                last = None
                for h in range(2):
                    hsl = slice(h * 64, (h + 1) * 64)
                    if h == 1:
                        t.drain()
                    last = t.matmul(pb[0][:, h * 128:(h + 1) * 128],
                                    dnKt[hsl, :], knv[hsl, csl],
                                    start=True, stop=True)
                    last = t.matmul(pb[0][:, 256 + h * 128:256 + (h + 1) * 128],
                                    knv[hsl, csl], dnKt[hsl, :],
                                    start=True, stop=True)
                    if c > 0:
                        last = t.matmul(pb[3][:, h * 64:(h + 1) * 64],
                                        dnKt[hsl, :], WTSB[hsl, :],
                                        start=True, stop=True)
                p.ct.tick(last)
            a_t = p.ct.val
            @block.vector
            def _(v, c=c):
                v.wait_ge(p.ct.sem, a_t)
                v.tensor_tensor(dnL, pb[0][:, 0:256], CONST["mlow"][:], ALU.mult)
                v.tensor_tensor(dnU, pb[0][:, 256:512], CONST["mup"][:], ALU.mult)
                if c > 0:
                    lv = v.tensor_tensor(dnX0, dnBV, pb[3][:, 0:128],
                                         ALU.subtract)
                else:
                    lv = v.tensor_copy(dnX0, dnBV)
                p.cv.tick(lv)
            lu_v = p.cv.val
            if CFGL.get("dn_stage", 9) < 3:
                continue
            prevU, prevL = dnU, dnL
            pw_v = lu_v
            for lvl in range(6):
                @block.tensor
                def _(t, prevU=prevU, prevL=prevL, pw_v=pw_v, lvl=lvl):
                    t.wait_ge(p.cv.sem, pw_v)
                    if lvl == 0:
                        t.drain()
                    last = None
                    for h in range(2):
                        hs2 = slice(h * 128, (h + 1) * 128)
                        last = t.matmul(pb[1][:, hs2], prevL[:, hs2],
                                        prevU[:, hs2], start=True, stop=True)
                        if lvl < 5:
                            last = t.matmul(pb[2][:, hs2], prevU[:, hs2],
                                            prevL[:, hs2], start=True, stop=True)
                    p.ct.tick(last)
                lt = p.ct.val
                @block.vector
                def _(v, lvl=lvl, lt=lt):
                    v.wait_ge(p.ct.sem, lt)
                    lv = v.tensor_copy(dnUpow[lvl], pb[1][:, 0:256])
                    if lvl < 5:
                        lv = v.tensor_copy(dnLpow[lvl], pb[2][:, 0:256])
                    p.cv.tick(lv)
                pw_v = p.cv.val
                prevU, prevL = dnUpow[lvl], dnLpow[lvl] if lvl < 5 else None
            if CFGL.get("dn_stage", 9) < 4:
                continue
            xcur = dnX0
            xnxt = dnX1
            sv_v = pw_v
            steps = [dnUpow[5], dnUpow[4], dnUpow[3], dnUpow[2], dnUpow[1],
                     dnUpow[0], dnU]
            for j, up in enumerate(steps):
                last = (j == len(steps) - 1)
                @block.tensor
                def _(t, up=up, xcur=xcur, sv_v=sv_v):
                    t.wait_ge(p.cv.sem, sv_v)
                    last = None
                    for h in range(2):
                        last = t.matmul(pb[4][:, h * 64:(h + 1) * 64],
                                        up[:, h * 128:(h + 1) * 128],
                                        xcur[:, h * 64:(h + 1) * 64],
                                        start=True, stop=True)
                    p.ct.tick(last)
                st_t = p.ct.val
                @block.vector
                def _(v, xcur=xcur, xnxt=xnxt, last=last, st_t=st_t):
                    v.wait_ge(p.ct.sem, st_t)
                    p.cv.tick(v.tensor_tensor(xnxt, xcur, pb[4][:, 0:128],
                                              ALU.subtract if last else ALU.add))
                sv_v = p.cv.val
                xcur, xnxt = xnxt, xcur
            if CFGL.get("dn_stage", 9) < 5:
                continue
            @block.tensor
            def _(t):
                t.wait_ge(p.cv.sem, p.cv.val)
                last = None
                for h in range(2):
                    hsl = slice(h * 64, (h + 1) * 64)
                    if h == 1:
                        t.drain()
                    last = t.matmul(pb[5][:, h * 128:(h + 1) * 128],
                                    knv[hsl, csl], qv[hsl, csl],
                                    start=True, stop=True)
                p.ct.tick(last)
            stt = p.ct.val
            @block.vector
            def _(v):
                v.wait_ge(p.ct.sem, stt)
                p.cv.tick(v.tensor_tensor(dnS, pb[5][:, 0:256],
                                          CONST["mupd"][:], ALU.mult))
            s_v = p.cv.val
            if CFGL.get("dn_stage", 9) < 6:
                continue
            @block.tensor
            def _(t, c=c, xcur=xcur):
                t.wait_ge(p.cv.sem, s_v)
                t.drain()
                h0, h1 = slice(0, 64), slice(64, 128)
                if c > 0:
                    t.matmul(pb[5][h0, 256:384], WTSB[h0, :],
                             qv[h0, csl], start=True, stop=False,
                             skip_group_check=True)
                    t.drain()
                    t.matmul(pb[5][h1, 256:384], WTSB[h1, :],
                             qv[h1, csl], start=True, stop=False,
                             skip_group_check=True)
                    t.drain()
                    for h, hsl in ((0, h0), (1, h1)):
                        t.matmul(pb[5][hsl, 256:384], xcur[:, hsl],
                                 dnS[:, h * 128:(h + 1) * 128],
                                 start=False, stop=True,
                                 skip_group_check=True)
                else:
                    for h, hsl in ((0, h0), (1, h1)):
                        t.matmul(pb[5][hsl, 256:384], xcur[:, hsl],
                                 dnS[:, h * 128:(h + 1) * 128],
                                 start=True, stop=True,
                                 skip_group_check=True)
                last = None
                for h, hsl in ((0, h0), (1, h1)):
                    last = t.matmul(pb[6][hsl, 0:64], dnKtm[:, hsl],
                                    xcur[:, hsl], start=(c == 0),
                                    stop=(c == NCH - 1), skip_group_check=True)
                p.ct.tick(last)
            o_t = p.ct.val
            @block.vector
            def _(v, c=c, g2=g % 2):
                v.wait_ge(p.ct.sem, o_t)
                v.tensor_copy(dn_oT(g2)[:, c * 128:(c + 1) * 128],
                              pb[5][:, 256:384])
                p.cv.tick(v.tensor_copy(WTSB[:], pb[6][:, 0:64]))

    def emit_dn_wout_pair(gpair):
        if p.done:
            return
        """X += oT(2) @ Wout rows for groups (2*gpair, 2*gpair+1)."""
        for dtile in range(NDT):
            for tc in range(NTC):
                bank = (dtile * NTC + tc) % 2
                @block.tensor
                def _(t, dtile=dtile, tc=tc, bank=bank):
                    t.wait_ge(p.cv.sem, p.cv.val)
                    for gi in range(2):
                        gg = 2 * gpair + gi
                        mm = t.matmul(
                            pb[bank][:],
                            WAR[:, WO0 + gg * 1024 + dtile * 128:
                                WO0 + gg * 1024 + dtile * 128 + 128],
                            dn_oT(gi)[:, tc * TCH:(tc + 1) * TCH],
                            start=(gi == 0), stop=(gi == 1))
                        if gi == 1:
                            p.ct.tick(mm)
                @block.vector
                def _(v, dtile=dtile, tc=tc, bank=bank):
                    v.wait_ge(p.ct.sem, p.ct.val)
                    xa = xsl(dtile, tc * TCH, TCH)
                    p.cv.tick(v.tensor_tensor(xa, xa, pb[bank][:], ALU.add))
        p.bar()

    def emit_dn(b, dnw_val):
        emit_beta(dnw_val)
        if CFGL.get("tap") == "dn_beta" and not p.done:
            dump(BHM[0:16, :], 2048, nparts=16)
            p.done = True
            return
        for g in range(CFGL.get("dn_ngroups", NG)):
            emit_dn_group(g)
            if CFGL.get("tap") == "dn_proj" and not p.done:
                dump(dn_kn(), 2048)
                p.done = True
                return
            if CFGL.get("tap") == "dn_oT" and not p.done:
                p.bar()
                dump(dn_oT(g % 2), 2048)
                p.done = True
                return
            if g % 2 == 1:
                p.bar()
                emit_dn_wout_pair(g // 2)

    SSUM = BTM[:, 0:1]
    SREC = BTM[:, 1:2]
    q_rotv = lambda ws: ASCB[:, ws * 2048:(ws + 1) * 2048]
    k_rollv = lambda s: ASCB[:, 4096 + s * 2048:4096 + (s + 1) * 2048]
    v_rollv = lambda s: ASC16[:, s * 2048:(s + 1) * 2048]
    oTv = ASC16
    hbufv = lambda ic: ASC16[:, ic * 512:(ic + 1) * 512]

    def emit_attn_weights(i):
        @block.gpsimd
        def _(g):
            g.wait_ge(p.ct.sem, p.ct.val)
            g.wait_ge(p.dV.sem, p.dV.val)
            for nm, off in [("st_wq", WQ0), ("st_wk", WK0), ("st_wv", WV0),
                            ("st_wo", WO0)]:
                p.dV.tick(g.dma_start(
                    out=WAR[:, off:off + 8192],
                    in_=dt[nm][i].rearrange("p a b -> p (a b)")))
        return p.dV.val

    def emit_layer_params(i):
        @block.sync
        def _(sy):
            sy.wait_ge(p.cv.sem, p.cv.val)
            sy.wait_ge(p.cs.sem, p.cs.val)
            p.dP.tick(sy.dma_start(out=B1A[:], in_=dt["st_b1a"][i]))
            p.dP.tick(sy.dma_start(out=B1G[:], in_=dt["st_b1g"][i]))
            p.dP.tick(sy.dma_start(out=LNG[:], in_=dt["st_lng"][i]))
            p.dP.tick(sy.dma_start(out=LNB[:], in_=dt["st_lnb"][i]))
            p.dP.tick(sy.dma_start(out=B2C[:], in_=dt["st_b2"][i]))
            p.dP.tick(sy.dma_start(out=QSC[:], in_=dt["st_qs"][i]))
            p.dP.tick(sy.dma_start(out=KSC[:], in_=dt["st_ks"][i]))
        return p.dP.val

    def emit_qk_pipeline(wofs, sc_ap, cs_tbl, ss_tbl, is_q, c, wv_val):
        if p.done:
            return
        """Project+l2norm+scale+rotary for q or k, one tchunk, all groups."""
        sl = slice(c * TCH, (c + 1) * TCH)
        q_raw = SCR[:, 0:512]
        sq = SCR[:, 512:1024]
        qn = SCRB[:, 0:512]
        tmp = SCRB[:, 512:1024]
        for g in range(NG):
            if p.done:
                return
            @block.tensor
            def _(t, g=g):
                t.wait_ge(p.cv.sem, p.cv.val)
                if g == 0:
                    t.wait_ge(p.dV.sem, wv_val)
                for dtile in range(NDT):
                    mm = t.matmul(
                        pb[0][:],
                        WAR[:, wofs + dtile * 1024 + g * 128:
                            wofs + dtile * 1024 + g * 128 + 128],
                        xsl(dtile, c * TCH, TCH, dtile >= 4),
                        start=(dtile == 0), stop=(dtile == NDT - 1))
                p.ct.tick(mm)
            @block.vector
            def _(v):
                v.wait_ge(p.ct.sem, p.ct.val)
                v.tensor_copy(q_raw, pb[0][:])
                v.drain()
                p.cv.tick(v.tensor_tensor(sq, q_raw, q_raw, ALU.mult))
            @block.tensor
            def _(t):
                t.wait_ge(p.cv.sem, p.cv.val)
                p.ct.tick(t.matmul(pb[2][0:2, :], CONST["sel2"][:], sq,
                                   start=True, stop=True))
            @block.scalar
            def _(s):
                s.wait_ge(p.ct.sem, p.ct.val)
                s.activation(ROWS[0:2, 0:512], pb[2][0:2, :], AF.Ln)
                s.drain()
                p.cs.tick(s.activation(ROWS[0:2, 0:512], ROWS[0:2, 0:512],
                                       AF.Exp, scale=-0.5))
            @block.vector
            def _(v):
                v.wait_ge(p.cs.sem, p.cs.val)
                p.cv.tick(v.tensor_scalar_min(KRB[:], ROWS[0:2, 0:512], 60000.0))
            @block.tensor
            def _(t):
                t.wait_ge(p.cv.sem, p.cv.val)
                p.ct.tick(t.matmul(pb[3][:], CONST["ind2h"][:], KRB[:],
                                   start=True, stop=True))
            @block.vector
            def _(v):
                v.wait_ge(p.ct.sem, p.ct.val)
                p.cv.tick(v.scalar_tensor_tensor(qn, q_raw, sc_ap, pb[3][:],
                                                 ALU.mult, ALU.mult))
            if CFGL.get("tap") == "qn0" and is_q and c == 0 and g == 0 and not p.done:
                dump(qn, 512)
                p.done = True
                return
            if CFGL.get("tap") == "qraw0" and is_q and c == 0 and g == 0 and not p.done:
                dump(q_raw, 512)
                p.done = True
                return
            @block.tensor
            def _(t):
                t.wait_ge(p.cv.sem, p.cv.val)
                p.ct.tick(t.matmul(pb[4][:], CONST["rsgnT"][:], qn,
                                   start=True, stop=True))
            @block.vector
            def _(v):
                v.wait_ge(p.ct.sem, p.ct.val)
                p.cv.tick(v.tensor_tensor(tmp, pb[4][:], ss_tbl, ALU.mult))
            @block.gpsimd
            def _(gp):
                gp.wait_ge(p.cv.sem, p.cv.val)
                p.cp.tick(gp.tensor_tensor(qn, qn, cs_tbl, ALU.mult))
            @block.vector
            def _(v, g=g):
                v.wait_ge(p.cp.sem, p.cp.val)
                for half in range(2):
                    w = 2 * c + half
                    dsl = (q_rotv(w % 2) if is_q
                           else k_rollv(w % 3))[:, g * 256:(g + 1) * 256]
                    lv = v.tensor_tensor(dsl, qn[:, half * 256:(half + 1) * 256],
                                         tmp[:, half * 256:(half + 1) * 256],
                                         ALU.add)
                p.cv.tick(lv)

    def emit_attn_proj(i, c, wv_val):
        if p.done:
            return
        """v/q/k projections + rotary for token chunk c."""
        @block.sync
        def _(sy):
            sy.wait_ge(p.cv.sem, p.cv.val)
            for j, nm in enumerate(("csq", "ssq", "csk", "ssk")):
                p.dX.tick(sy.dma_start(out=TBL[:, j * 512:(j + 1) * 512],
                                       in_=dt[nm][:, c * TCH:(c + 1) * TCH]))
        tbl_val = p.dX.val
        for tt in range(4):
            t0 = c * TCH + tt * 128
            b_abs = 2 * c * 2 + tt
            blk = c * 4 + tt
            slot = (blk // 2) % 3
            ttis = blk % 2
            for dvh in range(2):
                bank = (tt * 2 + dvh) % 2
                @block.tensor
                def _(t, t0=t0, dvh=dvh, bank=bank):
                    t.wait_ge(p.cv.sem, p.cv.val)
                    if c == 0 and t0 == 0 and dvh == 0:
                        t.wait_ge(p.dV.sem, wv_val)
                    last = None
                    for dtile in range(NDT):
                        last = t.matmul(pb[bank][:],
                                        xsl(dtile, t0, 128, dtile >= 4),
                                        WAR[:, WV0 + dtile * 1024 + dvh * 512:
                                            WV0 + dtile * 1024 + (dvh + 1) * 512],
                                        start=(dtile == 0),
                                        stop=(dtile == NDT - 1))
                    p.ct.tick(last)
                @block.vector
                def _(v, slot=slot, ttis=ttis, dvh=dvh, bank=bank):
                    v.wait_ge(p.ct.sem, p.ct.val)
                    p.cv.tick(v.tensor_copy(
                        v_rollv(slot)[:, ttis * 1024 + dvh * 512:
                                      ttis * 1024 + (dvh + 1) * 512],
                        pb[bank][:]))
        @block.vector
        def _(v):
            v.wait_ge(p.dX.sem, tbl_val)
        emit_qk_pipeline(WQ0, QSC[:, 0:1], TBL[:, 0:512], TBL[:, 512:1024],
                         True, c, wv_val)
        if p.done:
            return
        emit_qk_pipeline(WK0, KSC[:, 0:1], TBL[:, 1024:1536], TBL[:, 1536:2048],
                         False, c, wv_val)

    def emit_attn_window(i, w):
        if p.done:
            return
        ws = w % 2
        if w > 0:
            ncq = 384
            nptc = 768
            trmap = [[(0, 0, 0, 128), (1, 128, 128, 128), (2, 384, 256, 128)],
                     [(1, 256, 0, 128), (2, 512, 128, 128), (3, 640, 256, 128)]]
            pvmap = [(1, 128, 256, 0), (2, 384, 256, 0),
                     (0, 0, 128, 0), (3, 640, 128, 128)]
        else:
            ncq = 256
            nptc = 512
            trmap = [[(0, 0, 0, 128), (1, 256, 128, 128)],
                     [(0, 128, 0, 128), (1, 384, 128, 128)]]
            pvmap = [(0, 0, 256, 0), (1, 256, 256, 0)]
        kblocks = [2 * (w - 1) + kc if w > 0 else kc for kc in range(4)]
        kmap = [((b // 2) % 3, b % 2) for b in kblocks[:4 if w > 0 else 2]]
        sa = (w - 1) % 3 if w > 0 else 0
        sbx = w % 3 if w > 0 else 0
        P_sl = [SCR[:, j * 512:(j + 1) * 512] for j in range(2)]
        PT_p = SCR[:, 2048:3072]
        NEGM = [BTM[:, 8 + j:9 + j] for j in range(2)]
        SS = [BTM[:, 12 + j:13 + j] for j in range(2)]
        SR = [BTM[:, 16 + j:17 + j] for j in range(2)]
        NR = 16
        win0_cv = p.cv.val
        sim_t = [None] * NR
        mm_v = [None] * NR
        exp_s = [None] * NR
        nm_v = [None] * NR
        tr_t = [None] * NR
        cp_v = [None] * NR
        pv_t = [None] * NR
        ev_v = [None] * NR

        def emit_sim(r):
            h = r // 8
            g = r % 8
            hsl = slice(h * 64, (h + 1) * 64)
            bk = 2 + 2 * (r % 2)
            @block.tensor
            def _(t, r=r, h=h, g=g, hsl=hsl, bk=bk):
                if r < 2:
                    t.wait_ge(p.cv.sem, win0_cv)
                else:
                    t.wait_ge(p.cs.sem, exp_s[r - 2])
                if h == 1:
                    t.drain()
                last = None
                if w == 0:
                    for qt in range(2):
                        last = t.matmul(pb[bk + qt][:, 0:256],
                                        q_rotv(ws)[hsl, g * 256 + qt * 128:
                                                   g * 256 + qt * 128 + 128],
                                        k_rollv(0)[hsl, g * 256:(g + 1) * 256],
                                        start=True, stop=True)
                else:
                    q0 = q_rotv(ws)[hsl, g * 256:g * 256 + 128]
                    q1 = q_rotv(ws)[hsl, g * 256 + 128:g * 256 + 256]
                    t.matmul(pb[bk][:, 0:256], q0,
                             k_rollv(sa)[hsl, g * 256:(g + 1) * 256],
                             start=True, stop=False)
                    t.matmul(pb[bk][:, 256:384], q0,
                             k_rollv(sbx)[hsl, g * 256:g * 256 + 128],
                             start=False, stop=True)
                    t.matmul(pb[bk + 1][:, 0:128], q1,
                             k_rollv(sa)[hsl, g * 256 + 128:g * 256 + 256],
                             start=True, stop=False)
                    last = t.matmul(pb[bk + 1][:, 128:384], q1,
                                    k_rollv(sbx)[hsl, g * 256:(g + 1) * 256],
                                    start=False, stop=True)
                p.ct.tick(last)
            sim_t[r] = p.ct.val

        emit_sim(0)
        emit_sim(1)
        for r in range(NR):
            h = r // 8
            g = r % 8
            bk = 2 + 2 * (r % 2)
            @block.vector
            def _(v, r=r, bk=bk):
                v.wait_ge(p.ct.sem, sim_t[r])
                if w == 0:
                    v.tensor_tensor(pb[bk][:, 0:256], pb[bk][:, 0:256],
                                    CONST["maskB0"][:], ALU.add)
                    v.tensor_tensor(pb[bk + 1][:, 0:256], pb[bk + 1][:, 0:256],
                                    CONST["maskB1"][:], ALU.add)
                else:
                    v.tensor_tensor(pb[bk][:, 0:384], pb[bk][:, 0:384],
                                    CONST["maskA0"][:, 0:384], ALU.add)
                    v.tensor_tensor(pb[bk + 1][:, 0:384], pb[bk + 1][:, 0:384],
                                    CONST["maskA1"][:, 128:512], ALU.add)
                v.drain()
                for qt in range(2):
                    last = v.reduce_max(NEGM[qt], pb[bk + qt][:, 0:ncq],
                                        axis=AXX, negate=True)
                p.cv.tick(last)
            mm_v[r] = p.cv.val
            @block.scalar
            def _(s, r=r, bk=bk):
                s.wait_ge(p.cv.sem, mm_v[r])
                for qt in range(2):
                    last = s.activation(P_sl[qt][:, 0:ncq],
                                        pb[bk + qt][:, 0:ncq],
                                        AF.Exp, bias=NEGM[qt], accum_out=SS[qt])
                p.cs.tick(last)
            exp_s[r] = p.cs.val
            if r >= 2:
                @block.vector
                def _(v, r=r):
                    rr = r - 2
                    h2 = rr // 8
                    g2 = rr % 8
                    v.wait_ge(p.ct.sem, pv_t[rr])
                    p.cv.tick(v.tensor_copy(
                        oTv[h2 * 64:(h2 + 1) * 64,
                            6144 + g2 * 512 + ws * 256:
                            6144 + g2 * 512 + ws * 256 + 256],
                        pb[6][h2 * 64:(h2 + 1) * 64,
                              (rr % 2) * 256:(rr % 2) * 256 + 256]))
                ev_v[r - 2] = p.cv.val
            @block.vector
            def _(v, r=r):
                v.wait_ge(p.cs.sem, exp_s[r])
                v.reciprocal(SR[0], SS[0])
                v.reciprocal(SR[1], SS[1])
                v.drain()
                for qt in range(2):
                    last = v.tensor_scalar_mul(P_sl[qt][:, 0:ncq],
                                               P_sl[qt][:, 0:ncq], SR[qt])
                p.cv.tick(last)
            nm_v[r] = p.cv.val
            if r + 2 < NR:
                emit_sim(r + 2)
            @block.tensor
            def _(t, r=r):
                t.wait_ge(p.cv.sem, nm_v[r])
                if r >= 1:
                    t.wait_ge(p.cv.sem, cp_v[r - 1])
                if r + 2 < NR and (r + 2) // 8 == 1:
                    t.drain()
                for qt in range(2):
                    for kc, ptc, pc, wdt in trmap[qt]:
                        last = t.transpose(
                            pbT[:, ptc:ptc + wdt],
                            P_sl[qt][:, pc:pc + wdt],
                            CONST["ident16"][:])
                p.ct.tick(last)
            tr_t[r] = p.ct.val
            @block.vector
            def _(v, r=r):
                v.wait_ge(p.ct.sem, tr_t[r])
                p.cv.tick(v.tensor_copy(PT_p[:, 0:nptc], pbT[:, 0:nptc]))
            cp_v[r] = p.cv.val
            @block.tensor
            def _(t, r=r, h=h, g=g):
                t.wait_ge(p.cv.sem, cp_v[r])
                if r >= 2:
                    t.wait_ge(p.cv.sem, ev_v[r - 2])
                for n, (kc, ptc, wdt, oc) in enumerate(pvmap):
                    slot, ttis = kmap[kc]
                    last = t.matmul(
                        pb[6][h * 64:(h + 1) * 64,
                              (r % 2) * 256 + oc:(r % 2) * 256 + oc + wdt],
                        v_rollv(slot)[:, ttis * 1024 + (2 * g + h) * 64:
                                      ttis * 1024 + (2 * g + h) * 64 + 64],
                        PT_p[:, ptc:ptc + wdt],
                        start=(n == 0), stop=(n == len(pvmap) - 1),
                        skip_group_check=True)
                p.ct.tick(last)
            pv_t[r] = p.ct.val
        for rr in (NR - 2, NR - 1):
            @block.vector
            def _(v, rr=rr):
                h2 = rr // 8
                g2 = rr % 8
                v.wait_ge(p.ct.sem, pv_t[rr])
                p.cv.tick(v.tensor_copy(
                    oTv[h2 * 64:(h2 + 1) * 64,
                        6144 + g2 * 512 + ws * 256:
                        6144 + g2 * 512 + ws * 256 + 256],
                    pb[6][h2 * 64:(h2 + 1) * 64,
                          (rr % 2) * 256:(rr % 2) * 256 + 256]))
            ev_v[rr] = p.cv.val
        if w % 2 == 0:
            return
        t0 = (w - 1) * 256
        for dtile in range(NDT):
            bank = dtile % 2
            @block.tensor
            def _(t, dtile=dtile, bank=bank):
                t.wait_ge(p.cv.sem, p.cv.val)
                last = None
                for gb in range(NG):
                    last = t.matmul(pb[bank][:, 0:512],
                                    WAR[:, WO0 + gb * 1024 + dtile * 128:
                                        WO0 + gb * 1024 + dtile * 128 + 128],
                                    oTv[:, 6144 + gb * 512:6144 + (gb + 1) * 512],
                                    start=(gb == 0), stop=(gb == NG - 1))
                p.ct.tick(last)
            @block.vector
            def _(v, dtile=dtile, bank=bank):
                v.wait_ge(p.ct.sem, p.ct.val)
                xa = xsl(dtile, t0, 512)
                p.cv.tick(v.tensor_tensor(xa, xa, pb[bank][:, 0:512], ALU.add))

    def emit_attn_layer(i, wv_val, prm_val):
        @block.vector
        def _(v):
            v.wait_ge(p.dP.sem, prm_val)
        for c in range(NTC):
            if p.done:
                return
            emit_attn_proj(i, c, wv_val)
            if p.done:
                return
            p.bar()
            if CFGL.get("tap") == "qrot" and not p.done:
                dump(ASCB[:, 0:4096], 4096)
                p.done = True
                return
            if CFGL.get("tap") == "krot" and not p.done:
                dump(ASCB[:, 4096:8192], 4096)
                p.done = True
                return
            if CFGL.get("tap") == "vtm" and not p.done:
                dump(ASC16[:, 0:4096], 4096)
                p.done = True
                return
            emit_attn_window(i, 2 * c)
            p.bar()
            if p.done:
                return
            if CFGL.get("tap") == "oT0" and not p.done:
                dump(ASC16[:, 6144:8192], 2048)
                p.done = True
                return
            emit_attn_window(i, 2 * c + 1)
            p.bar()

    def emit_ffn_layer(i, prm_val):
        if p.done:
            return
        @block.vector
        def _(v):
            v.wait_ge(p.dP.sem, prm_val)
        emit_ln(1, "xn", (LNG, LNB))
        ic0 = 0
        for nchunk, icn in enumerate(FFN_CHUNKS):
            @block.gpsimd
            def _(g, ic0=ic0, icn=icn):
                g.wait_ge(p.ct.sem, p.ct.val)
                g.wait_ge(p.dV.sem, p.dV.val)
                p.dV.tick(g.dma_start(
                    out=WAR[:, 0:icn * 1024],
                    in_=dt["st_w1a"][i, :, ic0:ic0 + icn].rearrange(
                        "p a b c -> p (a b c)")))
                p.dV.tick(g.dma_start(
                    out=WAR[:, 8192:8192 + icn * 1024],
                    in_=dt["st_w1g"][i, :, ic0:ic0 + icn].rearrange(
                        "p a b c -> p (a b c)")))
                p.dV.tick(g.dma_start(
                    out=WAR[:, 16384:16384 + icn * 1024],
                    in_=dt["st_w2"][i, :, ic0:ic0 + icn].rearrange(
                        "p a b c -> p (a b c)")))
            wv = p.dV.val
            for c in range(NTC):
                sl = slice(c * TCH, (c + 1) * TCH)
                hv = [None] * icn
                gel = [None] * icn
                agt = [None] * icn
                for ici in range(icn):
                    ic = ic0 + ici
                    pa = pb[0] if ici % 2 == 0 else pb[2]
                    pg = pb[1] if ici % 2 == 0 else pb[3]
                    gsc = SCR[:, (ici % 2) * 512:(ici % 2) * 512 + 512]
                    @block.tensor
                    def _(t, ici=ici, c=c, pa=pa, pg=pg):
                        if ici < 2:
                            t.wait_ge(p.cv.sem, p.cv.val)
                            t.wait_ge(p.cs.sem, p.cs.val)
                            if ici == 0 and c == 0:
                                t.wait_ge(p.dV.sem, wv)
                        else:
                            t.wait_ge(p.cv.sem, hv[ici - 2])
                            t.wait_ge(p.cs.sem, gel[ici - 2])
                        for dtile in range(NDT):
                            t.matmul(pa[:],
                                     WAR[:, ici * 1024 + dtile * 128:
                                         ici * 1024 + dtile * 128 + 128],
                                     xnsl(dtile, c * TCH, TCH),
                                     start=(dtile == 0), stop=(dtile == NDT - 1))
                        last = None
                        for dtile in range(NDT):
                            last = t.matmul(
                                pg[:],
                                WAR[:, 8192 + ici * 1024 + dtile * 128:
                                    8192 + ici * 1024 + dtile * 128 + 128],
                                xnsl(dtile, c * TCH, TCH),
                                start=(dtile == 0), stop=(dtile == NDT - 1))
                        p.ct.tick(last)
                    agt[ici] = p.ct.val
                    @block.scalar
                    def _(s, ic=ic, ici=ici, pg=pg, gsc=gsc):
                        s.wait_ge(p.ct.sem, agt[ici])
                        if ici >= 2:
                            s.wait_ge(p.cv.sem, hv[ici - 2])
                        p.cs.tick(s.activation(gsc, pg[:], AF.Gelu,
                                               bias=B1G[:, ic:ic + 1]))
                    gel[ici] = p.cs.val
                    @block.vector
                    def _(v, ic=ic, ici=ici, pa=pa, gsc=gsc):
                        v.wait_ge(p.cs.sem, gel[ici])
                        if ici == 0:
                            v.wait_ge(p.ct.sem, p.ct.val)
                        p.cv.tick(v.scalar_tensor_tensor(
                            hbufv(ici), pa[:], B1A[:, ic:ic + 1], gsc,
                            ALU.add, ALU.mult))
                    hv[ici] = p.cv.val
                w2v = [None] * NDT
                for dtile in range(NDT):
                    bank = 4 + dtile % 2
                    @block.tensor
                    def _(t, dtile=dtile, bank=bank, icn=icn):
                        if dtile < 2:
                            t.wait_ge(p.cv.sem, hv[icn - 1])
                        else:
                            t.wait_ge(p.cv.sem, w2v[dtile - 2])
                        last = None
                        for ici in range(icn):
                            last = t.matmul(
                                pb[bank][:],
                                WAR[:, 16384 + ici * 1024 + dtile * 128:
                                    16384 + ici * 1024 + dtile * 128 + 128],
                                hbufv(ici),
                                start=(ici == 0), stop=(ici == icn - 1))
                        p.ct.tick(last)
                    w2t = p.ct.val
                    @block.vector
                    def _(v, dtile=dtile, bank=bank, last=(nchunk == 2), c=c,
                          w2t=w2t):
                        v.wait_ge(p.ct.sem, w2t)
                        xa = xsl(dtile, c * TCH, TCH)
                        if last:
                            lv = v.scalar_tensor_tensor(xa, pb[bank][:],
                                                        B2C[:, dtile:dtile + 1],
                                                        xa, ALU.add, ALU.add)
                        else:
                            lv = v.tensor_tensor(xa, xa, pb[bank][:], ALU.add)
                        p.cv.tick(lv)
                    w2v[dtile] = p.cv.val
            ic0 += icn
        p.bar()

    def emit_park(b):
        if p.done:
            return
        @block.sync
        def _(sy):
            sy.wait_ge(p.cv.sem, p.cv.val)
            sy.wait_ge(p.dW.sem, p.dW.val)
            for dtile in range(NDT):
                p.dK.tick(sy.dma_start(out=x1park[b, dtile * 128:(dtile + 1) * 128, :],
                                       in_=xsl(dtile, 0, T)))
        return p.dK.val

    def emit_final(b):
        if p.done:
            return
        for dtile in range(NDT):
            for c in range(NTC):
                @block.sync
                def _(sy, dtile=dtile, c=c):
                    sy.wait_ge(p.cv.sem, p.cv.val)
                    p.dK.tick(sy.dma_start(
                        out=X1S[:],
                        in_=x1park[b, dtile * 128:(dtile + 1) * 128,
                                   c * TCH:(c + 1) * TCH]))
                @block.vector
                def _(v, dtile=dtile, c=c):
                    v.wait_ge(p.dK.sem, p.dK.val)
                    xa = xsl(dtile, c * TCH, TCH)
                    p.cv.tick(v.tensor_tensor(xa, xa, X1S[:], ALU.add))
                @block.gpsimd
                def _(g, dtile=dtile, c=c):
                    g.wait_ge(p.cv.sem, p.cv.val)
                    p.dW.tick(g.dma_start(
                        out=xout[b, dtile * 128:(dtile + 1) * 128,
                                 c * TCH:(c + 1) * TCH],
                        in_=xsl(dtile, c * TCH, TCH)))
        @block.vector
        def _(v):
            v.wait_ge(p.dW.sem, p.dW.val)

    for b in range(CFGL["batches"]):
        if p.done:
            break
        xv = emit_load_x(b)
        dnw = emit_dn_weights()
        emit_ln(0, "xn", None, xload_val=xv)
        tap("ln1", f=lambda: dump(XN[:, 0:6656], 6656))
        if p.done:
            break
        if CFGL["do_dn"]:
            emit_dn(b, dnw)
            tap("dn_x1", f=lambda: dump(xsl(0, 0, 2048), 2048))
            if p.done:
                break
        park_val = emit_park(b)
        emit_ln(0, "x", None, wait_dX=park_val)
        tap("ln2", f=lambda: dump(xsl(0, 0, 2048), 2048))
        if p.done:
            break
        for i in range(CFGL["layers"]):
            if p.done:
                break
            wv = emit_attn_weights(i)
            prm = emit_layer_params(i)
            emit_attn_layer(i, wv, prm)
            tap(f"L{i}a", f=lambda: dump(xsl(0, 0, 2048), 2048))
            if p.done:
                break
            emit_ffn_layer(i, prm)
            tap(f"L{i}f", f=lambda: dump(xsl(0, 0, 2048), 2048))
            if p.done:
                break
        if p.done:
            break
        emit_final(b)

    es.close()
    return nc, dt



def kernel(**inputs):
    x = np.asarray(inputs["L0_attn_x"], np.float32)
    B = x.shape[0]
    xr = np.ascontiguousarray(
        x.reshape(8, 2, T, D).transpose(0, 1, 3, 2))
    w = _prep_weights(inputs)
    c = _consts()
    nc, _ = build_program(CFG)
    in_maps = []
    for core in range(8):
        m = {"xin": xr[core]}
        m.update(w)
        m.update(c)
        in_maps.append(m)
    res = run_bass_kernel_spmd(nc, in_maps, core_ids=list(range(8)))
    outs = [r["xout"] for r in res.results]
    out = np.stack(outs, 0).reshape(16, D, T).transpose(0, 2, 1)
    return np.ascontiguousarray(out)


def _debug_run(tap_name, core=0, inputs=None):
    CFG["tap"] = tap_name
    x = np.asarray(inputs["L0_attn_x"], np.float32)
    xr = np.ascontiguousarray(
        x.reshape(8, 2, T, D).transpose(0, 1, 3, 2))
    w = _prep_weights(inputs)
    c = _consts()
    nc, _ = build_program(CFG)
    m = {"xin": xr[core]}
    m.update(w)
    m.update(c)
    res = run_bass_kernel_spmd(nc, [m], core_ids=[0])
    CFG["tap"] = None
    return res.results[0]
